# version 19
# baseline (speedup 1.0000x reference)
import numpy as np
import ml_dtypes
import concourse.bass as bass
import concourse.mybir as mybir
from concourse.bass_utils import run_bass_kernel_spmd
from contextlib import ExitStack

F32 = mybir.dt.float32
BF16 = mybir.dt.bfloat16
AF = mybir.ActivationFunctionType
ALU = mybir.AluOpType

D, L, E, I, N = 2048, 2048, 4096, 8192, 16
C = 512
NCH = L // C
DB, EB, IB, IC, TB = D // 128, E // 128, I // 128, I // C, C // 128
EPS = 1e-5

ENGS = ("tensor", "vector", "scalar", "gpsimd", "sync")
DEBUG = False


class KB:
    def __init__(self, nc, stack):
        self.nc = nc
        self.stack = stack
        self.ops = {e: [] for e in ENGS}
        self.sem = {}
        self.cnt = {}
        self.hw = {e: {} for e in ENGS}

    def mksem(self, *names):
        for name in names:
            self.sem[name] = self.stack.enter_context(self.nc.semaphore(name))
            self.cnt[name] = 0

    def wait(self, eng, sname, val=None):
        val = self.cnt[sname] if val is None else val
        if val <= 0 or val <= self.hw[eng].get(sname, 0):
            return
        self.hw[eng][sname] = val
        h = self.sem[sname]
        self.ops[eng].append(lambda X, h=h, v=val: X.wait_ge(h, v))

    def emit(self, eng, fn, inc=None, add=1):
        if inc is None:
            self.ops[eng].append(lambda X, fn=fn: fn(X))
            return None
        self.cnt[inc] += add
        h = self.sem[inc]
        self.ops[eng].append(lambda X, fn=fn, h=h, a=add: fn(X).then_inc(h, a))
        return self.cnt[inc]

    DMA_OWNERS = {"gpsimd": ("s_x", "s_bx", "s_sm", "wsA", "wsB", "wpA", "wpB", "wpC", "wpD",
                             "s_us", "s_ss", "s_xr"),
                  "sync": ("s_st", "s_out", "s_bw",
                           "wsA", "wsB", "wpA", "wpB", "wpC", "wpD",
                           "s_xl", "s_ud", "s_sd")}

    def barrier(self):
        for e, sems in KB.DMA_OWNERS.items():
            for s in sems:
                self.wait(e, s)
        tgt = self.cnt["ph"] + len(ENGS)
        h = self.sem["ph"]
        for e in ENGS:
            self.ops[e].append(lambda X, h=h: X.sem_inc(h, 1))
        self.cnt["ph"] = tgt
        for e in ENGS:
            self.wait(e, "ph", tgt)


def build_program():
    nc = bass.Bass("TRN2", target_bir_lowering=False, dynamic_dma_scratch_size=16384)
    stack = ExitStack()
    k = KB(nc, stack)
    k.mksem("ph", "s_x", "s_st", "s_out", "s_bw", "s_bx",
            "wsA", "wsB", "wpA", "wpB", "wpC", "wpD", "s_sm",
            "c_pe", "c_dv", "c_sc", "c_gp",
            "s_xl", "s_ud", "s_sd", "s_us", "s_ss", "s_xr")

    def din(name, shape, dt=F32):
        return nc.dram_tensor(name, shape, dt, kind="ExternalInput").ap()

    x_fm = din("x_fm", [D, L])
    wg_sl = din("wg_sl", [IC, 128, 16 * C])
    wu_sl = din("wu_sl", [IC, 128, 16 * C])
    wd_sl = din("wd_sl", [DB, 128, IB * 128])
    wsk_sl = din("wsk_sl", [EB, 128, DB * 128])
    win_sl = din("win_sl", [EB, 128, DB * 128])
    wo_sl = din("wo_sl", [DB, 128, EB * 128])
    wd1_in = din("wd1_in", [128, EB * 128])
    wd2_in = din("wd2_in", [128, E])
    wb_in = din("wb_in", [128, EB * N])
    wc_in = din("wc_in", [128, EB * N])
    cw_in = din("cw_in", [128, EB * 4])
    cb_in = din("cb_in", [128, EB])
    bd2_in = din("bd2_in", [128, EB])
    wD_in = din("wD_in", [128, EB])
    nw_in = din("nw_in", [128, DB])
    alog_in = din("alog_in", [128, EB * N])
    consts_in = din("consts_in", [128, 8])
    ident_in = nc.dram_tensor("ident_in", [128, 128], BF16, kind="ExternalInput").ap()
    onescol_in = nc.dram_tensor("onescol_in", [128, 1], BF16, kind="ExternalInput").ap()
    onesrow_in = din("onesrow_in", [1, 128])
    onesrowb_in = nc.dram_tensor("onesrowb_in", [1, 128], BF16, kind="ExternalInput").ap()

    out_d = nc.dram_tensor("out", [D, L], F32, kind="ExternalOutput").ap()
    _scr = "ExternalOutput" if DEBUG else "Internal"
    if DEBUG:
        dbg_xn = nc.dram_tensor("dbg_xn", [NCH, 128, DB * C], BF16, kind="ExternalOutput").ap()
        dbg_hb = nc.dram_tensor("dbg_hb", [NCH, 128, C], F32, kind="ExternalOutput").ap()
        dbg_ab = nc.dram_tensor("dbg_ab", [NCH, 128, C], F32, kind="ExternalOutput").ap()
        dbg_s14 = nc.dram_tensor("dbg_s14", [NCH, 128, C], F32, kind="ExternalOutput").ap()
        dbg_hsp = nc.dram_tensor("dbg_hsp", [NCH, 128, C], BF16, kind="ExternalOutput").ap()
        dbg_hspt = nc.dram_tensor("dbg_hspt", [NCH, 128, IB * C], BF16, kind="ExternalOutput").ap()
        dbg_scl = nc.dram_tensor("dbg_scl", [NCH, 128, C], F32, kind="ExternalOutput").ap()
        dbg_u = nc.dram_tensor("dbg_u", [NCH, 128, EB * C], BF16, kind="ExternalOutput").ap()
        dbg_vt = nc.dram_tensor("dbg_vt", [NCH, 128, EB * 3], F32, kind="ExternalOutput").ap()
        dbg_row = nc.dram_tensor("dbg_row", [NCH, 4, C], F32, kind="ExternalOutput").ap()
    stream1 = nc.dram_tensor("stream1", [NCH, 128, DB * C], BF16, kind=_scr).ap()
    stream2 = nc.dram_tensor("stream2", [NCH, 128, DB * C], BF16, kind=_scr).ap()
    bounce = nc.dram_tensor("bounce", [NCH, 2, N, C], BF16, kind=_scr).ap()
    wgb_d = nc.dram_tensor("wgb_d", [IC, 128, 16 * C], BF16, kind="Internal").ap()
    wub_d = nc.dram_tensor("wub_d", [IC, 128, 16 * C], BF16, kind="Internal").ap()
    wdb_d = nc.dram_tensor("wdb_d", [DB, 128, IB * 128], BF16, kind="Internal").ap()
    wskb_d = nc.dram_tensor("wskb_d", [EB, 128, DB * 128], BF16, kind="Internal").ap()
    winb_d = nc.dram_tensor("winb_d", [EB, 128, DB * 128], BF16, kind="Internal").ap()
    wob_d = nc.dram_tensor("wob_d", [DB, 128, EB * 128], BF16, kind="Internal").ap()
    u_d = nc.dram_tensor("u_d", [NCH, 128, EB * C], BF16, kind="Internal").ap()
    sk_d = nc.dram_tensor("sk_d", [NCH, 128, EB * C], BF16, kind="Internal").ap()

    RA = lambda name, shape, dt: stack.enter_context(nc.sbuf_tensor(name, shape, dt))
    ident = RA("ident", [128, 128], BF16)
    onescol = RA("onescol", [128, 1], BF16)
    onesrow = RA("onesrow", [1, 128], F32)
    onesrowb = RA("onesrowb", [1, 128], BF16)
    consts = RA("consts", [128, 8], F32)
    convw = RA("convw", [128, EB * 4], F32)
    convb = RA("convb", [128, EB], F32)
    bd2c = RA("bd2c", [128, EB], F32)
    wDc = RA("wDc", [128, EB], F32)
    nwc = RA("nwc", [128, DB], F32)
    aneg = RA("aneg", [128, EB * N], F32)
    carry = RA("carry", [128, EB * N], F32)
    vtail = RA("vtail", [128, EB * 3], F32)
    d1bf = RA("d1bf", [128, C], BF16)
    wbt = RA("wbt", [128, EB * N], BF16)
    wct = RA("wct", [128, EB * N], BF16)
    s1f = RA("s1f", [1, C], F32)
    sclf = RA("sclf", [1, C], F32)
    t1f = RA("t1f", [1, C], F32)
    t2f = RA("t2f", [1, C], F32)
    mean_rep = RA("mean_rep", [128, C], F32)
    scl_rep = RA("scl_rep", [128, C], F32)

    CE = lambda j: consts[:, j:j + 1]
    CE1 = lambda j: consts[0:1, j:j + 1]
    g = "gpsimd"

    for dst, src in [(ident, ident_in), (onescol, onescol_in), (onesrow, onesrow_in),
                     (onesrowb, onesrowb_in), (consts, consts_in), (convw, cw_in),
                     (convb, cb_in), (bd2c, bd2_in), (wDc, wD_in), (nwc, nw_in),
                     (wbt, wb_in), (wct, wc_in), (carry, alog_in)]:
        k.emit(g, lambda X, d=dst, s=src: X.dma_start(d[:], s), "s_sm", 16)
    k.wait("scalar", "s_sm")
    k.emit("scalar", lambda X: X.activation(aneg[:], carry[:], AF.Exp), "c_sc")
    k.wait("vector", "c_sc")
    k.emit("vector", lambda X: X.tensor_single_scalar(aneg[:], aneg[:], -1.0, ALU.mult), "c_dv")
    k.emit("vector", lambda X: X.memset(carry[:], 0.0), "c_dv")
    k.emit("vector", lambda X: X.memset(vtail[:], 0.0), "c_dv")
    k.barrier()

    def ln_stats(xck, sq, nslot, ps_a1, ps_b1, ps_r1, ps_r2, rms_extra):
        k.wait("tensor", "c_dv")
        k.wait("tensor", "c_sc")
        sq_tok = [0] * DB
        for db in range(DB):
            sl = db % nslot
            if db >= nslot:
                k.wait("scalar", "c_pe", sq_tok[db - nslot])
            k.emit("scalar", lambda X, db=db, sl=sl: X.activation(
                sq[:, sl * C:(sl + 1) * C], xck[:, db * C:(db + 1) * C], AF.Square), "c_sc")
            k.wait("tensor", "c_sc")
            k.emit("tensor", lambda X, db=db: X.matmul(
                ps_a1, onescol[:], xck[:, db * C:(db + 1) * C],
                start=(db == 0), stop=(db == DB - 1)), "c_pe")
            sq_tok[db] = k.emit("tensor", lambda X, db=db, sl=sl: X.matmul(
                ps_b1, onescol[:], sq[:, sl * C:(sl + 1) * C],
                start=(db == 0), stop=(db == DB - 1)), "c_pe")
        k.wait("vector", "c_pe")
        k.emit("vector", lambda X: X.tensor_single_scalar(s1f[:], ps_a1, 1.0 / D, ALU.mult), "c_dv")
        k.emit("vector", lambda X: X.tensor_single_scalar(t2f[:], ps_b1, 1.0 / D, ALU.mult), "c_dv")
        k.emit("vector", lambda X: X.tensor_mul(t1f[:], s1f[:], s1f[:]), "c_dv")
        k.emit("vector", lambda X: X.tensor_sub(t1f[:], t2f[:], t1f[:]), "c_dv")
        k.wait("scalar", "c_dv")
        k.emit("scalar", lambda X: X.activation(t2f[:], t1f[:], AF.Ln, bias=CE1(0)), "c_sc")
        k.emit("scalar", lambda X: X.activation(sclf[:], t2f[:], AF.Exp, scale=CE1(1)), "c_sc")
        if rms_extra:
            k.emit("scalar", lambda X: X.activation(t2f[:], t2f[:], AF.Exp, scale=CE1(3)), "c_sc")
            k.wait("vector", "c_sc")
            k.emit("vector", lambda X: X.tensor_mul(t2f[:], t2f[:], t1f[:]), "c_dv")
            k.wait("scalar", "c_dv")
            k.emit("scalar", lambda X: X.activation(t2f[:], t2f[:], AF.Ln, bias=CE1(0)), "c_sc")
            k.emit("scalar", lambda X: X.activation(t2f[:], t2f[:], AF.Exp, scale=CE1(1)), "c_sc")
            k.wait("vector", "c_sc")
            k.emit("vector", lambda X: X.tensor_mul(sclf[:], sclf[:], t2f[:]), "c_dv")
        k.wait("tensor", "c_sc")
        k.wait("tensor", "c_dv")
        k.emit("tensor", lambda X: X.matmul(ps_r1, onesrow[:], s1f[:], start=True, stop=True), "c_pe")
        k.emit("tensor", lambda X: X.matmul(ps_r2, onesrow[:], sclf[:], start=True, stop=True), "c_pe")
        k.wait("vector", "c_pe")
        k.emit("vector", lambda X: X.tensor_copy(mean_rep[:], ps_r1), "c_dv")
        k.emit("vector", lambda X: X.tensor_copy(scl_rep[:], ps_r2), "c_dv")
        return k.cnt["c_dv"]

    def ln_apply(xck, xnck, tmp, rep_tok, norm_w):
        k.wait("vector", "c_dv", rep_tok)
        for db in range(DB):
            k.emit("vector", lambda X, db=db: X.tensor_sub(
                tmp[:], xck[:, db * C:(db + 1) * C], mean_rep[:]), "c_dv")
            if norm_w:
                k.emit("vector", lambda X, db=db: X.scalar_tensor_tensor(
                    xnck[:, db * C:(db + 1) * C], tmp[:], nwc[:, db:db + 1], scl_rep[:],
                    ALU.mult, ALU.mult), "c_dv")
            else:
                k.emit("vector", lambda X, db=db: X.tensor_mul(
                    xnck[:, db * C:(db + 1) * C], tmp[:], scl_rep[:]), "c_dv")
        return k.cnt["c_dv"]

    def ffn_phase(first, pfx):
        ps = ExitStack()
        PA = lambda name, shape, dt: ps.enter_context(nc.sbuf_tensor(pfx + name, shape, dt))
        PP = lambda name, shape, dt=F32: ps.enter_context(nc.psum_tensor(pfx + name, shape, dt))
        ps_g = PP("ps_g", [128, C])
        ps_u = PP("ps_u", [128, C])
        ps_tr = PP("ps_tr", [128, C], BF16)
        ps_dA = PP("ps_dA", [128, C])
        ps_dB = PP("ps_dB", [128, C])
        ps_s1 = PP("ps_s1", [1, C])
        ps_s2 = PP("ps_s2", [1, C])
        ps_s3 = PP("ps_s3", [128, C])
        xck = PA("xck", [128, DB * C], BF16)
        xnck = PA("xnck", [128, DB * C], BF16)
        hspt = PA("hspt", [128, IB * C], BF16)
        wsbA = PA("wsbA", [128, 2 * 16 * C], BF16)
        wsbB = PA("wsbB", [128, 2 * 16 * C], BF16)
        sq2 = PA("sq2", [128, 2 * C], BF16)
        silg = PA("silg", [128, C], F32)
        hb = PA("hb", [128, C], F32)
        ab = PA("ab", [128, C], F32)
        mn4 = PA("mn4", [128, 4 * 128], F32)
        s14k = PA("s14k", [128, 4 * 128], F32)
        s14 = s14k
        kp = s14k[:, 384:512]
        hsp = PA("hsp", [128, 2 * C], BF16)
        ob = None if first else PA("ob", [128, 2 * C], F32)

        ws = [wsbA, wsbB]
        wsem = ["wsA", "wsB"]
        ws_tok = [0, 0]
        psd = [ps_dA, ps_dB]
        psd_tok = [0, 0]
        psg2 = [ps_g, ps_s3]
        psu2 = [ps_u, ps_dB]
        psg_tok2 = [0, 0]
        psu_tok2 = [0, 0]
        psg_tok = psu_tok = pstr_tok = 0
        hspdone_prev = 0
        hsp_tok = [0, 0]
        ob_tok = [0, 0]
        resid_done = 0
        pend = []
        gu_last = 0
        st_tok = 0

        for c in range(NCH):
            k.wait(g, "c_dv", resid_done)
            if first:
                src = x_fm[:, c * C:(c + 1) * C].rearrange("(db p) ii -> p db ii", p=128)
                k.emit(g, lambda X, src=src: X.dma_start(
                    xck[:].rearrange("p (db ii) -> p db ii", db=DB), src), "s_x", 16)
            else:
                k.emit(g, lambda X, c=c: X.dma_start(xck[:], stream2[c]), "s_x", 16)
            xtok = k.cnt["s_x"]
            for e in ("scalar", "tensor", "vector"):
                k.wait(e, "s_x", xtok)

            rep_tok = ln_stats(xck, sq2, 2, ps_s1[:], ps_s2[:], ps_s3[:], ps_dA[:],
                               rms_extra=False)
            k.wait("vector", "c_pe", gu_last)
            if first:
                k.wait("vector", "s_st", st_tok)
            apply_tok = ln_apply(xck, xnck, hb, rep_tok, norm_w=False)
            if DEBUG and first:
                k.wait("sync", "c_dv", apply_tok)
                k.emit("sync", lambda X, c=c: X.dma_start(dbg_xn[c], xnck[:]), "s_st", 16)
                k.emit("sync", lambda X, c=c: X.dma_start(dbg_scl[c], scl_rep[:]), "s_st", 16)
                k.emit("sync", lambda X, c=c: X.dma_start(dbg_row[c, 0:1], s1f[:]), "s_st", 16)
                k.emit("sync", lambda X, c=c: X.dma_start(dbg_row[c, 1:2], sclf[:]), "s_st", 16)
                k.emit("sync", lambda X, c=c: X.dma_start(dbg_row[c, 2:3], t1f[:]), "s_st", 16)
                k.emit("sync", lambda X, c=c: X.dma_start(dbg_row[c, 3:4], t2f[:]), "s_st", 16)

            for ic in range(IC):
                sl = ic % 2
                if first and c == 0:
                    k.wait(g, wsem[sl])
                    k.wait(g, "c_pe", ws_tok[sl])
                    k.emit(g, lambda X, ic=ic, sl=sl: X.dma_start(
                        ws[sl][:, 0:16 * C], wg_sl[ic]), wsem[sl], 16)
                    k.emit(g, lambda X, ic=ic, sl=sl: X.dma_start(
                        ws[sl][:, 16 * C:32 * C], wu_sl[ic]), wsem[sl], 16)
                    wtok = k.cnt[wsem[sl]]
                    k.wait("sync", wsem[sl], wtok)
                    k.emit("sync", lambda X, ic=ic, sl=sl: X.dma_start(
                        wgb_d[ic], ws[sl][:, 0:16 * C]), wsem[sl], 16)
                    k.emit("sync", lambda X, ic=ic, sl=sl: X.dma_start(
                        wub_d[ic], ws[sl][:, 16 * C:32 * C]), wsem[sl], 16)
                else:
                    k.wait("sync", wsem[sl])
                    k.wait("sync", "c_pe", ws_tok[sl])
                    k.emit("sync", lambda X, ic=ic, sl=sl: X.dma_start(
                        ws[sl][:, 0:16 * C], wgb_d[ic]), wsem[sl], 16)
                    k.emit("sync", lambda X, ic=ic, sl=sl: X.dma_start(
                        ws[sl][:, 16 * C:32 * C], wub_d[ic]), wsem[sl], 16)
                    wtok = k.cnt[wsem[sl]]
                k.wait("tensor", wsem[sl], wtok)
                k.wait("tensor", "c_dv", apply_tok)
                for tb in range(TB):
                    gp = (ic * TB + tb) % 2
                    psg = psg2[gp]
                    psu = psu2[gp]
                    k.wait("tensor", "c_sc", psg_tok2[gp])
                    for kb in range(DB):
                        k.emit("tensor", lambda X, kb=kb, tb=tb, sl=sl, psg=psg: X.matmul(
                            psg[:], xnck[:, kb * C + tb * 128: kb * C + tb * 128 + 128],
                            ws[sl][:, kb * C:(kb + 1) * C],
                            start=(kb == 0), stop=(kb == DB - 1)), "c_pe")
                    gdone = k.cnt["c_pe"]
                    k.wait("tensor", "c_dv", psu_tok2[gp])
                    k.wait("tensor", "c_dv", psd_tok[1])
                    for kb in range(DB):
                        k.emit("tensor", lambda X, kb=kb, tb=tb, sl=sl, psu=psu: X.matmul(
                            psu[:], xnck[:, kb * C + tb * 128: kb * C + tb * 128 + 128],
                            ws[sl][:, 16 * C + kb * C: 16 * C + (kb + 1) * C],
                            start=(kb == 0), stop=(kb == DB - 1)), "c_pe")
                    udone = k.cnt["c_pe"]
                    if tb == TB - 1:
                        ws_tok[sl] = udone
                        if ic == IC - 1:
                            gu_last = udone
                    while len(pend) > 1:
                        hsl_, ic_, tb_, tok_ = pend.pop(0)
                        k.wait("tensor", "c_dv", tok_)
                        k.wait("tensor", "c_sc", pstr_tok)
                        for q in range(4):
                            k.emit("tensor", lambda X, q=q, hsl_=hsl_: X.transpose(
                                ps_tr[:, q * 128:(q + 1) * 128],
                                hsp[:, hsl_ * C + q * 128: hsl_ * C + (q + 1) * 128],
                                ident[:]), "c_pe")
                        hsp_tok[hsl_] = k.cnt["c_pe"]
                        k.wait("scalar", "c_pe", hsp_tok[hsl_])
                        dst = hspt[:].rearrange("p (ib ii) -> p ib ii", ib=IB)
                        pstr_tok = k.emit("scalar", lambda X, dst=dst, ic_=ic_, tb_=tb_: X.copy(
                            dst[:, ic_ * 4:(ic_ + 1) * 4, tb_ * 128:(tb_ + 1) * 128],
                            ps_tr[:].rearrange("p (q ii) -> p q ii", q=4)), "c_sc")
                    k.wait("scalar", "c_pe", gdone)
                    k.wait("scalar", "c_dv", psu_tok)
                    psg_tok = k.emit("scalar", lambda X, psg=psg: X.activation(
                        silg[:], psg[:], AF.Silu), "c_sc")
                    psg_tok2[gp] = psg_tok
                    k.wait("vector", "c_pe", udone)
                    k.wait("vector", "c_sc", psg_tok)
                    k.emit("vector", lambda X, psu=psu: X.tensor_mul(hb[:], silg[:], psu[:]), "c_dv")
                    psu_tok = k.cnt["c_dv"]
                    psu_tok2[gp] = psu_tok
                    k.wait("scalar", "c_dv", psu_tok)
                    abs_tok = k.emit("scalar", lambda X: X.activation(
                        ab[:], hb[:], AF.Abs), "c_sc")
                    k.wait("vector", "c_sc", abs_tok)
                    a3 = ab[:].rearrange("p (n q) -> p n q", q=4)
                    h3 = hb[:].rearrange("p (n q) -> p n q", q=4)
                    q1 = lambda t: t.rearrange("p (n q) -> p n q", q=1)
                    k.emit("vector", lambda X, a3=a3: X.tensor_tensor(
                        q1(mn4[:, 0:128]), a3[:, :, 0:1], a3[:, :, 1:2], ALU.max), "c_dv")
                    k.emit("vector", lambda X, a3=a3: X.tensor_tensor(
                        q1(mn4[:, 128:256]), a3[:, :, 0:1], a3[:, :, 1:2], ALU.min), "c_dv")
                    k.emit("vector", lambda X, a3=a3: X.tensor_tensor(
                        q1(mn4[:, 256:384]), a3[:, :, 2:3], a3[:, :, 3:4], ALU.max), "c_dv")
                    k.emit("vector", lambda X, a3=a3: X.tensor_tensor(
                        q1(mn4[:, 384:512]), a3[:, :, 2:3], a3[:, :, 3:4], ALU.min), "c_dv")
                    k.emit("vector", lambda X: X.tensor_tensor(
                        s14[:, 0:128], mn4[:, 0:128], mn4[:, 256:384], ALU.min), "c_dv")
                    k.emit("vector", lambda X: X.tensor_tensor(
                        s14[:, 128:256], mn4[:, 128:256], mn4[:, 384:512], ALU.max), "c_dv")
                    k.emit("vector", lambda X: X.tensor_tensor(
                        s14[:, 256:384], s14[:, 0:128], s14[:, 128:256], ALU.max), "c_dv")
                    hsl = tb % 2
                    k.wait("vector", "c_pe", hsp_tok[hsl])
                    hs3 = hsp[:, hsl * C:(hsl + 1) * C].rearrange("p (n q) -> p n q", q=4)
                    for j in range(4):
                        k.emit("vector", lambda X, a3=a3, j=j: X.tensor_tensor(
                            q1(kp), a3[:, :, j:j + 1], q1(s14[:, 256:384]), ALU.is_ge), "c_dv")
                        k.emit("vector", lambda X, h3=h3, hs3=hs3, j=j: X.tensor_tensor(
                            hs3[:, :, j:j + 1], h3[:, :, j:j + 1], q1(kp), ALU.mult), "c_dv")
                    hspdone = k.cnt["c_dv"]
                    hspdone_prev = hspdone
                    pend.append((hsl, ic, tb, hspdone))

            def flush_tr():
                nonlocal pstr_tok
                while pend:
                    hsl_, ic_, tb_, tok_ = pend.pop(0)
                    k.wait("tensor", "c_dv", tok_)
                    k.wait("tensor", "c_sc", pstr_tok)
                    for q in range(4):
                        k.emit("tensor", lambda X, q=q, hsl_=hsl_: X.transpose(
                            ps_tr[:, q * 128:(q + 1) * 128],
                            hsp[:, hsl_ * C + q * 128: hsl_ * C + (q + 1) * 128],
                            ident[:]), "c_pe")
                    hsp_tok[hsl_] = k.cnt["c_pe"]
                    k.wait("scalar", "c_pe", hsp_tok[hsl_])
                    dst = hspt[:].rearrange("p (ib ii) -> p ib ii", ib=IB)
                    pstr_tok = k.emit("scalar", lambda X, dst=dst, ic_=ic_, tb_=tb_: X.copy(
                        dst[:, ic_ * 4:(ic_ + 1) * 4, tb_ * 128:(tb_ + 1) * 128],
                        ps_tr[:].rearrange("p (q ii) -> p q ii", q=4)), "c_sc")
            flush_tr()
            hspt_done = pstr_tok
            if DEBUG and first:
                k.wait("sync", "c_sc", hspt_done)
                k.emit("sync", lambda X, c=c: X.dma_start(dbg_hspt[c], hspt[:]), "s_st", 16)

            for db in range(DB):
                sl = db % 2
                if first and c == 0:
                    k.wait(g, wsem[sl])
                    k.wait(g, "c_pe", ws_tok[sl])
                    k.emit(g, lambda X, db=db, sl=sl: X.dma_start(
                        ws[sl][:, 0:IB * 128], wd_sl[db]), wsem[sl], 16)
                    wtok = k.cnt[wsem[sl]]
                    k.wait("sync", wsem[sl], wtok)
                    k.emit("sync", lambda X, db=db, sl=sl: X.dma_start(
                        wdb_d[db], ws[sl][:, 0:IB * 128]), wsem[sl], 16)
                else:
                    k.wait("sync", wsem[sl])
                    k.wait("sync", "c_pe", ws_tok[sl])
                    k.emit("sync", lambda X, db=db, sl=sl: X.dma_start(
                        ws[sl][:, 0:IB * 128], wdb_d[db]), wsem[sl], 16)
                    wtok = k.cnt[wsem[sl]]
                k.wait("tensor", wsem[sl], wtok)
                k.wait("tensor", "c_sc", hspt_done)
                k.wait("tensor", "c_dv", psd_tok[sl])
                if sl == 1:
                    k.wait("tensor", "c_dv", psu_tok2[1])
                for ib in range(IB):
                    k.emit("tensor", lambda X, ib=ib, sl=sl: X.matmul(
                        psd[sl][:], ws[sl][:, ib * 128:(ib + 1) * 128],
                        hspt[:, ib * C:(ib + 1) * C],
                        start=(ib == 0), stop=(ib == IB - 1)), "c_pe")
                ws_tok[sl] = k.cnt["c_pe"]
                k.wait("vector", "c_pe", ws_tok[sl])
                if first:
                    k.emit("vector", lambda X, db=db, sl=sl: X.scalar_tensor_tensor(
                        xnck[:, db * C:(db + 1) * C], psd[sl][:], 0.5,
                        xck[:, db * C:(db + 1) * C], ALU.mult, ALU.add), "c_dv")
                else:
                    osl = db % 2
                    k.wait("vector", "s_out", ob_tok[osl])
                    k.emit("vector", lambda X, db=db, sl=sl, osl=osl: X.scalar_tensor_tensor(
                        ob[:, osl * C:(osl + 1) * C], psd[sl][:], 0.5,
                        xck[:, db * C:(db + 1) * C], ALU.mult, ALU.add), "c_dv")
                    odone = k.cnt["c_dv"]
                    k.wait("sync", "c_dv", odone)
                    dsto = out_d[db * 128:(db + 1) * 128, c * C:(c + 1) * C]
                    ob_tok[osl] = k.emit("sync", lambda X, dsto=dsto, osl=osl: X.dma_start(
                        dsto, ob[:, osl * C:(osl + 1) * C]), "s_out", 16)
                psd_tok[sl] = k.cnt["c_dv"]
            resid_done = k.cnt["c_dv"]
            if first:
                k.wait("sync", "c_dv", resid_done)
                st_tok = k.emit("sync", lambda X, c=c: X.dma_start(
                    stream1[c], xnck[:]), "s_st", 16)
        ps.close()

    ffn_phase(first=True, pfx="p1_")
    k.barrier()

    p2 = ExitStack()
    PA = lambda name, shape, dt: p2.enter_context(nc.sbuf_tensor(name, shape, dt))
    PP = lambda name, shape, dt=F32: p2.enter_context(nc.psum_tensor(name, shape, dt))
    ps_sk = PP("ps_sk", [128, C])
    ps_v = PP("ps_v", [128, C])
    ps_d1 = PP("ps_d1", [128, C])
    ps_bt = PP("ps_bt", [16, C])
    ps_ct = PP("ps_ct", [16, C])
    ps_m1 = PP("ps_m1", [128, C])
    ps_m2 = PP("ps_m2", [128, C])
    ps_m3 = PP("ps_m3", [1, C])

    xnb = PA("xnb", [128, DB * C], BF16)
    brep = PA("brep", [128, N * C], BF16)
    crep = PA("crep", [128, N * C], BF16)
    dch = PA("dch", [128, EB * C], BF16)
    ych = PA("ych", [128, EB * C], BF16)
    wq0 = PA("wq0", [128, EB * 128], BF16)
    wq1 = PA("wq1", [128, EB * 128], BF16)
    wq2 = PA("wq2", [128, DB * 128], BF16)
    wq3 = wq2
    wd1t = PA("wd1t", [128, EB * 128], BF16)
    wd2t = PA("wd2t", [128, E], BF16)
    k.emit(g, lambda X: X.dma_start(wd1t[:], wd1_in), "s_sm", 16)
    k.emit(g, lambda X: X.dma_start(wd2t[:], wd2_in), "s_sm", 16)
    k.wait("tensor", "s_sm")
    vstage = PA("vstage", [128, C + 3], F32)
    uca = PA("uca", [128, C], F32)
    sq2b = PA("sq2b", [128, 2 * C], BF16)
    tmpc = PA("tmpc", [128, C], F32)
    tmpd = PA("tmpd", [128, C], F32)
    dub = PA("dub", [128, C], BF16)
    abuf = PA("abuf", [128, C], F32)
    bbuf = PA("bbuf", [128, C], BF16)
    hbuf = PA("hbuf", [128, 2 * C], F32)
    zbuf = PA("zbuf", [128, 2 * C], BF16)
    ybufA = PA("ybufA", [128, C], BF16)
    ybufB = PA("ybufB", [128, C], BF16)
    brr = PA("brr", [1, 2 * C], BF16)
    xst = PA("xst", [128, 2 * C], BF16)
    ustg = PA("ustg", [128, 1 * C], BF16)
    skstg = PA("skstg", [128, 1 * C], BF16)
    u_s = PA("u_s", [128, 2 * C], BF16)
    sk_s = PA("sk_s", [128, 2 * C], BF16)
    x1r = PA("x1r", [128, 1 * C], BF16)
    x2s = PA("x2s", [128, 1 * C], BF16)
    btf = PA("btf", [16, C], BF16)
    ctf = PA("ctf", [16, C], BF16)

    wp = [wq0, wq1, wq2, wq2]
    wpsem = ["wpA", "wpB", "wpC", "wpD"]
    wp_tok = [0, 0, 0, 0]
    wv_tok = [0, 0]
    brr_tok = [0, 0]
    sk_tok = 0
    v_tok = 0
    u_tok = 0
    ustg_mm = [0, 0]
    ustg_st = [0, 0]
    skstg_st = [0, 0]
    us_tok = [0, 0]
    ss_tok = [0, 0]
    xr_tok = [0, 0]
    x2_tok = [0, 0]
    xstp_tok = [0, 0]
    xstd_tok = [0, 0]
    scan_tok = [0, 0]
    z_tok2 = [0, 0]
    y_tok2 = [0, 0]
    carry_tok = [0, 0]
    ab2 = [abuf, tmpd]
    yb = [ybufA, ybufB]
    a1_mm_last = 0
    wout_last = 0
    d1ev_tok = 0
    dmm_tok = 0
    rep_done = 0
    scan_last_dv = 0
    conv_last_dv = 0
    dch_rdy = 0
    ln_done_dv = [0]
    dch_eb = [0] * EB
    b_tok_n = [0] * N
    z_tok_n = [0] * N
    brep_rdy = 0
    crep_rdy = 0
    yp_done = [0] * NCH

    def emit_ln(c):
        if True:
            k.wait("tensor", "c_dv")
            k.wait("tensor", "c_sc")
            sqtok = [0] * DB
            for db in range(DB):
                sl = db % 2
                k.wait("sync", "c_pe", xstp_tok[sl])
                k.wait("sync", "c_dv", xstd_tok[sl])
                k.emit("sync", lambda X, c=c, db=db, sl=sl: X.dma_start(
                    xst[:, sl * C:(sl + 1) * C],
                    stream1[c][:, db * C:(db + 1) * C]), "s_xl", 16)
                xl = k.cnt["s_xl"]
                k.wait("scalar", "s_xl", xl)
                if db >= 2:
                    k.wait("scalar", "c_pe", sqtok[db - 2])
                k.emit("scalar", lambda X, sl=sl, db=db: X.activation(
                    sq2b[:, (db % 2) * C:(db % 2 + 1) * C],
                    xst[:, sl * C:(sl + 1) * C], AF.Square), "c_sc")
                k.wait("tensor", "s_xl", xl)
                k.wait("tensor", "c_sc")
                k.emit("tensor", lambda X, sl=sl, db=db: X.matmul(
                    ps_m3[:], onescol[:], xst[:, sl * C:(sl + 1) * C],
                    start=(db == 0), stop=(db == DB - 1)), "c_pe")
                sqtok[db] = k.emit("tensor", lambda X, db=db: X.matmul(
                    ps_m1[0:1, :], onescol[:], sq2b[:, (db % 2) * C:(db % 2 + 1) * C],
                    start=(db == 0), stop=(db == DB - 1)), "c_pe")
                xstp_tok[sl] = sqtok[db]
            k.wait("vector", "c_pe", sqtok[DB - 1])
            k.emit("vector", lambda X: X.tensor_single_scalar(s1f[:], ps_m3[:], 1.0 / D, ALU.mult), "c_dv")
            k.emit("vector", lambda X: X.tensor_single_scalar(t2f[:], ps_m1[0:1, :], 1.0 / D, ALU.mult), "c_dv")
            k.emit("vector", lambda X: X.tensor_mul(t1f[:], s1f[:], s1f[:]), "c_dv")
            k.emit("vector", lambda X: X.tensor_sub(t1f[:], t2f[:], t1f[:]), "c_dv")
            k.wait("scalar", "c_dv")
            k.emit("scalar", lambda X: X.activation(t2f[:], t1f[:], AF.Ln, bias=CE1(0)), "c_sc")
            k.emit("scalar", lambda X: X.activation(sclf[:], t2f[:], AF.Exp, scale=CE1(1)), "c_sc")
            k.emit("scalar", lambda X: X.activation(t2f[:], t2f[:], AF.Exp, scale=CE1(3)), "c_sc")
            k.wait("vector", "c_sc")
            k.emit("vector", lambda X: X.tensor_mul(t2f[:], t2f[:], t1f[:]), "c_dv")
            k.wait("scalar", "c_dv")
            k.emit("scalar", lambda X: X.activation(t2f[:], t2f[:], AF.Ln, bias=CE1(0)), "c_sc")
            k.emit("scalar", lambda X: X.activation(t2f[:], t2f[:], AF.Exp, scale=CE1(1)), "c_sc")
            k.wait("vector", "c_sc")
            k.emit("vector", lambda X: X.tensor_mul(sclf[:], sclf[:], t2f[:]), "c_dv")
            k.wait("tensor", "c_dv")
            k.emit("tensor", lambda X: X.matmul(ps_m1[:], onesrow[:], s1f[:], start=True, stop=True), "c_pe")
            bc_tok = k.emit("tensor", lambda X: X.matmul(ps_m2[:], onesrow[:], sclf[:], start=True, stop=True), "c_pe")
            k.wait("vector", "c_pe", bc_tok)
            k.emit("vector", lambda X: X.tensor_copy(mean_rep[:], ps_m1[:]), "c_dv")
            k.emit("vector", lambda X: X.tensor_copy(scl_rep[:], ps_m2[:]), "c_dv")
            ln_done_dv[0] = k.cnt["c_dv"]
            k.wait("vector", "c_pe", a1_mm_last)
            for db in range(DB):
                sl = db % 2
                k.wait("sync", "c_pe", xstp_tok[sl])
                k.wait("sync", "c_dv", xstd_tok[sl])
                k.emit("sync", lambda X, c=c, db=db, sl=sl: X.dma_start(
                    xst[:, sl * C:(sl + 1) * C],
                    stream1[c][:, db * C:(db + 1) * C]), "s_xl", 16)
                xl = k.cnt["s_xl"]
                k.wait("vector", "s_xl", xl)
                k.emit("vector", lambda X, sl=sl: X.tensor_sub(
                    tmpc[:], xst[:, sl * C:(sl + 1) * C], mean_rep[:]), "c_dv")
                k.emit("vector", lambda X, db=db: X.scalar_tensor_tensor(
                    xnb[:, db * C:(db + 1) * C], tmpc[:], nwc[:, db:db + 1], scl_rep[:],
                    ALU.mult, ALU.mult), "c_dv")
                xstd_tok[sl] = k.cnt["c_dv"]
            return k.cnt["c_dv"]

    apply_tok = emit_ln(0)
    for c in range(NCH + 1):
        cm1 = c - 1
        for eb in range(EB):
            if c < NCH:
                sl = eb % 2
                if c == 0:
                    k.wait(g, wpsem[sl])
                    k.wait(g, "c_pe", wp_tok[sl])
                    k.emit(g, lambda X, eb=eb, sl=sl: X.dma_start(
                        wp[sl][:, 0:DB * 128], wsk_sl[eb]), wpsem[sl], 16)
                    sk_w = k.cnt[wpsem[sl]]
                    k.wait("sync", wpsem[sl], sk_w)
                    k.emit("sync", lambda X, eb=eb, sl=sl: X.dma_start(
                        wskb_d[eb], wp[sl][:, 0:DB * 128]), wpsem[sl], 16)
                else:
                    k.wait("sync", wpsem[sl])
                    k.wait("sync", "c_pe", wp_tok[sl])
                    k.emit("sync", lambda X, eb=eb, sl=sl: X.dma_start(
                        wp[sl][:, 0:DB * 128], wskb_d[eb]), wpsem[sl], 16)
                    sk_w = k.cnt[wpsem[sl]]
                if c == 0:
                    k.wait(g, wpsem[2])
                    k.wait(g, "c_pe", max(wp_tok[2], wp_tok[3]))
                    k.emit(g, lambda X, eb=eb, sl=sl: X.dma_start(
                        wp[2][:], win_sl[eb]), wpsem[2], 16)
                    in_w = k.cnt[wpsem[2]]
                    k.wait("sync", wpsem[2], in_w)
                    k.emit("sync", lambda X, eb=eb: X.dma_start(
                        winb_d[eb], wp[2][:]), wpsem[2], 16)
                else:
                    k.wait("sync", wpsem[2])
                    k.wait("sync", "c_pe", max(wp_tok[2], wp_tok[3]))
                    k.emit("sync", lambda X, eb=eb: X.dma_start(
                        wp[2][:], winb_d[eb]), wpsem[2], 16)
                    in_w = k.cnt[wpsem[2]]
                k.wait("tensor", wpsem[sl], sk_w)
                k.wait("tensor", "c_dv", apply_tok)
                k.wait("tensor", "c_sc", sk_tok)
                for kb in range(DB):
                    k.emit("tensor", lambda X, kb=kb, sl=sl: X.matmul(
                        ps_sk[:], wp[sl][:, kb * 128:(kb + 1) * 128],
                        xnb[:, kb * C:(kb + 1) * C],
                        start=(kb == 0), stop=(kb == DB - 1)), "c_pe")
                wp_tok[sl] = k.cnt["c_pe"]
                k.wait("scalar", "c_pe", wp_tok[sl])
                k.wait("scalar", "s_sd", skstg_st[0])
                sk_tok = k.emit("scalar", lambda X, sl=sl: X.activation(
                    skstg[:, 0:C], ps_sk[:], AF.Silu), "c_sc")
                k.wait("sync", "c_sc", sk_tok)
                skstg_st[0] = k.emit("sync", lambda X, c=c, eb=eb: X.dma_start(
                    sk_d[c][:, eb * C:(eb + 1) * C],
                    skstg[:, 0:C]), "s_sd", 16)
                k.wait("tensor", wpsem[2], in_w)
                k.wait("tensor", "c_dv", v_tok)
                for kb in range(DB):
                    k.emit("tensor", lambda X, kb=kb: X.matmul(
                        ps_v[:], wp[2][:, kb * 128:(kb + 1) * 128],
                        xnb[:, kb * C:(kb + 1) * C],
                        start=(kb == 0), stop=(kb == DB - 1)), "c_pe")
                wp_tok[2] = k.cnt["c_pe"]
                k.wait("vector", "c_pe", wp_tok[2])
                k.emit("vector", lambda X, eb=eb: X.tensor_copy(
                    vstage[:, 0:3], vtail[:, eb * 3:(eb + 1) * 3]), "c_dv")
                k.emit("vector", lambda X, eb=eb: X.tensor_copy(
                    vtail[:, eb * 3:(eb + 1) * 3], ps_v[:, C - 3:C]), "c_dv")
                k.emit("vector", lambda X: X.tensor_copy(vstage[:, 3:C + 3], ps_v[:]), "c_dv")
                v_tok = k.cnt["c_dv"]
                k.wait("vector", "c_sc", u_tok)
                k.emit("vector", lambda X, eb=eb: X.tensor_scalar(
                    uca[:], vstage[:, 3:C + 3], convw[:, eb * 4 + 3:eb * 4 + 4], None,
                    ALU.mult), "c_dv")
                k.emit("vector", lambda X, eb=eb: X.scalar_tensor_tensor(
                    tmpc[:], vstage[:, 2:C + 2], convw[:, eb * 4 + 2:eb * 4 + 3], uca[:],
                    ALU.mult, ALU.add), "c_dv")
                k.emit("vector", lambda X, eb=eb: X.scalar_tensor_tensor(
                    uca[:], vstage[:, 1:C + 1], convw[:, eb * 4 + 1:eb * 4 + 2], tmpc[:],
                    ALU.mult, ALU.add), "c_dv")
                k.emit("vector", lambda X, eb=eb: X.scalar_tensor_tensor(
                    tmpc[:], vstage[:, 0:C], convw[:, eb * 4:eb * 4 + 1], uca[:],
                    ALU.mult, ALU.add), "c_dv")
                conv_last_dv = k.cnt["c_dv"]
                k.wait("scalar", "c_dv", conv_last_dv)
                k.wait("scalar", "c_pe", ustg_mm[0])
                k.wait("scalar", "s_ud", ustg_st[0])
                u_tok = k.emit("scalar", lambda X, eb=eb, sl=sl: X.activation(
                    ustg[:, 0:C], tmpc[:], AF.Silu,
                    bias=convb[:, eb:eb + 1]), "c_sc")
                k.wait("sync", "c_sc", u_tok)
                ustg_st[0] = k.emit("sync", lambda X, c=c, eb=eb: X.dma_start(
                    u_d[c][:, eb * C:(eb + 1) * C],
                    ustg[:, 0:C]), "s_ud", 16)
                k.wait("tensor", "c_sc", u_tok)
                k.emit("tensor", lambda X, eb=eb, sl=sl: X.matmul(
                    ps_d1[:], wd1t[:, eb * 128:(eb + 1) * 128],
                    ustg[:, 0:C],
                    start=(eb == 0), stop=(eb == EB - 1)), "c_pe")
                k.emit("tensor", lambda X, eb=eb, sl=sl: X.matmul(
                    ps_bt[:], wbt[:, eb * N:(eb + 1) * N],
                    ustg[:, 0:C],
                    start=(eb == 0), stop=(eb == EB - 1)), "c_pe")
                k.emit("tensor", lambda X, eb=eb, sl=sl: X.matmul(
                    ps_ct[:], wct[:, eb * N:(eb + 1) * N],
                    ustg[:, 0:C],
                    start=(eb == 0), stop=(eb == EB - 1)), "c_pe")
                ustg_mm[0] = k.cnt["c_pe"]
                if eb == EB - 1:
                    a1_mm_last = k.cnt["c_pe"]

            if c > 0:
                ssl = eb % 2
                k.wait(g, "c_dv", us_tok[ssl])
                k.emit(g, lambda X, cm1=cm1, eb=eb, ssl=ssl: X.dma_start(
                    u_s[:, ssl * C:(ssl + 1) * C],
                    u_d[cm1][:, eb * C:(eb + 1) * C]), "s_us", 16)
                ul = k.cnt["s_us"]
                k.wait(g, "c_dv", ss_tok[ssl])
                k.emit(g, lambda X, cm1=cm1, eb=eb, ssl=ssl: X.dma_start(
                    sk_s[:, ssl * C:(ssl + 1) * C],
                    sk_d[cm1][:, eb * C:(eb + 1) * C]), "s_ss", 16)
                skl = k.cnt["s_ss"]
                if eb == 0:
                    k.wait("vector", "s_bx", brep_rdy)
                    k.wait(g, "s_x", crep_rdy)
                k.wait("scalar", "c_dv", dch_eb[eb])
                k.wait("vector", "s_us", ul)
                k.emit("vector", lambda X, eb=eb, ssl=ssl: X.tensor_mul(
                    dub[:], dch[:, eb * C:(eb + 1) * C],
                    u_s[:, ssl * C:(ssl + 1) * C]), "c_dv")

                def scan_post(n):
                    hsl2 = n % 2
                    k.wait("vector", "c_gp", z_tok2[hsl2])
                    if n == 0:
                        k.emit("vector", lambda X, hsl2=hsl2: X.tensor_copy(
                            ybufA[:], zbuf[:, hsl2 * C:(hsl2 + 1) * C]), "c_dv")
                    else:
                        k.emit("vector", lambda X, n=n, hsl2=hsl2: X.tensor_add(
                            yb[n % 2][:], yb[1 - n % 2][:],
                            zbuf[:, hsl2 * C:(hsl2 + 1) * C]), "c_dv")
                    y_tok2[hsl2] = k.cnt["c_dv"]

                def z_emit(n, delay_tok):
                    asl = n % 2
                    k.wait(g, "c_dv", scan_tok[asl])
                    k.wait(g, "c_dv", delay_tok)
                    k.wait(g, "c_dv", y_tok2[asl])
                    z_tok2[asl] = k.emit(g, lambda X, n=n, asl=asl: X.tensor_tensor(
                        zbuf[:, asl * C:(asl + 1) * C], hbuf[:, asl * C:(asl + 1) * C],
                        crep[:, n * C:(n + 1) * C], ALU.mult), "c_gp")
                    if eb == EB - 1:
                        z_tok_n[n] = z_tok2[asl]

                def carry_emit(n):
                    asl = n % 2
                    col = eb * N + n
                    k.wait("scalar", "c_dv", scan_tok[asl])
                    carry_tok[asl] = k.emit("scalar", lambda X, col=col, asl=asl: X.copy(
                        carry[:, col:col + 1], hbuf[:, asl * C + C - 1: asl * C + C]), "c_sc")

                for n in range(N):
                    col = eb * N + n
                    asl = n % 2
                    k.wait("scalar", "c_dv", scan_tok[asl])
                    a_sc = k.emit("scalar", lambda X, eb=eb, col=col, asl=asl: X.activation(
                        ab2[asl][:], dch[:, eb * C:(eb + 1) * C], AF.Exp,
                        scale=aneg[:, col:col + 1]), "c_sc")
                    k.emit("vector", lambda X, n=n: X.tensor_mul(
                        bbuf[:], dub[:], brep[:, n * C:(n + 1) * C]), "c_dv")
                    bn_tok = k.cnt["c_dv"]
                    if eb == EB - 1:
                        b_tok_n[n] = bn_tok
                    k.wait("vector", "c_sc", a_sc)
                    k.wait("vector", "c_gp", z_tok2[asl])
                    k.wait("vector", "c_sc", carry_tok[asl])
                    scan_tok[asl] = k.emit("vector", lambda X, col=col, asl=asl: X.tensor_tensor_scan(
                        hbuf[:, asl * C:(asl + 1) * C], ab2[asl][:], bbuf[:],
                        carry[:, col:col + 1], ALU.mult, ALU.add), "c_dv")
                    k.wait("tensor", "c_dv", max(ln_done_dv[0], scan_tok[asl]))
                    k.emit("tensor", lambda X: X.matmul(
                        ps_m3[0:1, 0:8], onescol[:], ident[:, 0:8],
                        start=True, stop=True), "c_pe")
                    if n > 0:
                        z_emit(n - 1, bn_tok)
                        carry_emit(n - 1)
                    if n > 1:
                        scan_post(n - 2)
                z_emit(N - 1, scan_tok[(N - 1) % 2])
                carry_emit(N - 1)
                scan_post(N - 2)
                scan_post(N - 1)
                ylast = yb[(N - 1) % 2]
                k.emit("vector", lambda X, eb=eb, ssl=ssl, ylast=ylast: X.scalar_tensor_tensor(
                    zbuf[:, 0:C], u_s[:, ssl * C:(ssl + 1) * C], wDc[:, eb:eb + 1],
                    ylast[:], ALU.mult, ALU.add), "c_dv")
                us_tok[ssl] = k.cnt["c_dv"]
                k.wait("vector", "s_ss", skl)
                k.wait("vector", "c_pe", wout_last)
                k.emit("vector", lambda X, eb=eb, ssl=ssl: X.tensor_mul(
                    ych[:, eb * C:(eb + 1) * C], zbuf[:, 0:C],
                    sk_s[:, ssl * C:(ssl + 1) * C]), "c_dv")
                ss_tok[ssl] = k.cnt["c_dv"]
                if eb == EB - 1:
                    scan_last_dv = k.cnt["c_dv"]
                    yp_done[cm1] = scan_last_dv

        if c < NCH:
            k.wait("scalar", "c_pe", a1_mm_last)
            k.wait("scalar", "c_pe", dmm_tok)
            d1ev_tok = k.emit("scalar", lambda X: X.copy(d1bf[:], ps_d1[:]), "c_sc")
            k.wait("vector", "c_pe", a1_mm_last)
            k.emit("vector", lambda X: X.tensor_copy(btf[:], ps_bt[:]), "c_dv")
            k.emit("vector", lambda X: X.tensor_copy(ctf[:], ps_ct[:]), "c_dv")
            bt_tok = k.cnt["c_dv"]
            k.wait("sync", "c_dv", bt_tok)
            k.emit("sync", lambda X, c=c: X.dma_start(bounce[c, 0], btf[:]), "s_bw", 16)
            k.emit("sync", lambda X, c=c: X.dma_start(bounce[c, 1], ctf[:]), "s_bw", 16)
            bw_tok = k.cnt["s_bw"]

        if c < NCH:
            for n in range(N):
                k.wait(g, "s_bw", bw_tok)
                k.wait(g, "c_dv", b_tok_n[n])
                k.emit(g, lambda X, c=c, n=n: X.dma_start(
                    brep[:, n * C:(n + 1) * C],
                    bounce[c, 0, n].partition_broadcast(128)), "s_bx", 16)
            brep_rdy = k.cnt["s_bx"]
            for n in range(N):
                k.wait(g, "s_bw", bw_tok)
                k.wait(g, "c_gp", z_tok_n[n])
                k.emit(g, lambda X, c=c, n=n: X.dma_start(
                    crep[:, n * C:(n + 1) * C],
                    bounce[c, 1, n].partition_broadcast(128)), "s_x", 16)
            crep_rdy = k.cnt["s_x"]

            dt0 = hbuf[:, 0:C]
            dt1 = hbuf[:, C:2 * C]
            sp_dv = 0
            for eb in range(EB):
                k.wait("tensor", "c_sc", d1ev_tok)
                k.wait("tensor", "c_sc")
                k.wait("tensor", "c_dv", v_tok)
                k.emit("tensor", lambda X, eb=eb: X.matmul(
                    ps_v[:], wd2t[:, eb * 128:(eb + 1) * 128], d1bf[:],
                    start=True, stop=True), "c_pe")
                dmm_tok = k.cnt["c_pe"]
                k.wait("scalar", "c_pe", dmm_tok)
                k.wait("scalar", "c_gp", max(z_tok2))
                k.wait("scalar", "c_dv", max(sp_dv, scan_tok[0], scan_tok[1]))
                k.emit("scalar", lambda X, eb=eb, dt0=dt0: X.activation(
                    dt0, ps_v[:], AF.Relu, bias=bd2c[:, eb:eb + 1]), "c_sc")
                k.emit("scalar", lambda X, eb=eb, dt1=dt1: X.activation(
                    dt1, ps_v[:], AF.Abs, bias=bd2c[:, eb:eb + 1]), "c_sc")
                k.emit("scalar", lambda X, dt1=dt1: X.activation(dt1, dt1, AF.Exp, scale=CE(3)), "c_sc")
                sp_sc = k.emit("scalar", lambda X, dt1=dt1: X.activation(dt1, dt1, AF.Ln, bias=CE(4)), "c_sc")
                k.wait("vector", "c_sc", sp_sc)
                k.emit("vector", lambda X, eb=eb, dt0=dt0, dt1=dt1: X.tensor_add(
                    dch[:, eb * C:(eb + 1) * C], dt0, dt1), "c_dv")
                sp_dv = k.cnt["c_dv"]
                dch_eb[eb] = sp_dv
            dch_rdy = sp_dv

        if c > 0:
            pso = [ps_m1, ps_m2]
            pso_tok = [0, 0]
            for db in range(DB):
                sl = db % 2
                if cm1 == 0:
                    k.wait(g, wpsem[sl])
                    k.wait(g, "c_pe", wv_tok[sl])
                    k.emit(g, lambda X, db=db, sl=sl: X.dma_start(
                        wp[sl][:], wo_sl[db]), wpsem[sl], 16)
                    wtok = k.cnt[wpsem[sl]]
                    k.wait("sync", wpsem[sl], wtok)
                    k.emit("sync", lambda X, db=db, sl=sl: X.dma_start(
                        wob_d[db], wp[sl][:]), wpsem[sl], 16)
                else:
                    k.wait("sync", wpsem[sl])
                    k.wait("sync", "c_pe", wv_tok[sl])
                    k.emit("sync", lambda X, db=db, sl=sl: X.dma_start(
                        wp[sl][:], wob_d[db]), wpsem[sl], 16)
                    wtok = k.cnt[wpsem[sl]]
                k.wait("tensor", wpsem[sl], wtok)
                k.wait("tensor", "c_dv", yp_done[cm1])
                k.wait("tensor", "c_dv", pso_tok[sl])
                k.wait("tensor", "c_dv")
                for eb in range(EB):
                    k.emit("tensor", lambda X, eb=eb, sl=sl: X.matmul(
                        pso[sl][:], wp[sl][:, eb * 128:(eb + 1) * 128],
                        ych[:, eb * C:(eb + 1) * C],
                        start=(eb == 0), stop=(eb == EB - 1)), "c_pe")
                wv_tok[sl] = k.cnt["c_pe"]
                if db == DB - 1:
                    wout_last = k.cnt["c_pe"]
                k.wait(g, "c_dv", xr_tok[0])
                k.emit(g, lambda X, cm1=cm1, db=db, sl=sl: X.dma_start(
                    x1r[:, 0:C],
                    stream1[cm1][:, db * C:(db + 1) * C]), "s_xr", 16)
                xrl = k.cnt["s_xr"]
                k.wait("vector", "c_pe", wv_tok[sl])
                k.wait("vector", "s_xr", xrl)
                k.wait("vector", "s_st", x2_tok[0])
                k.emit("vector", lambda X, db=db, sl=sl: X.scalar_tensor_tensor(
                    x2s[:, 0:C], pso[sl][:], 1.0,
                    x1r[:, 0:C], ALU.bypass, ALU.add), "c_dv")
                xr_tok[0] = k.cnt["c_dv"]
                pso_tok[sl] = k.cnt["c_dv"]
                k.wait("sync", "c_dv", pso_tok[sl])
                x2_tok[0] = k.emit("sync", lambda X, cm1=cm1, db=db: X.dma_start(
                    stream2[cm1][:, db * C:(db + 1) * C],
                    x2s[:, 0:C]), "s_st", 16)

        if c + 1 < NCH:
            apply_tok = emit_ln(c + 1)

    p2.close()
    k.barrier()

    ffn_phase(first=False, pfx="p3_")
    k.wait("sync", "s_out")
    k.barrier()

    with nc.Block() as block:
        block.tensor(lambda X: [f(X) for f in k.ops["tensor"]])
        block.vector(lambda X: [f(X) for f in k.ops["vector"]])
        block.scalar(lambda X: [f(X) for f in k.ops["scalar"]])
        block.gpsimd(lambda X: [f(X) for f in k.ops["gpsimd"]])
        block.sync(lambda X: [f(X) for f in k.ops["sync"]])
    stack.close()
    return nc


def _prep_inputs(inp):
    f = np.float32
    ac = np.ascontiguousarray
    w_gate = np.asarray(inp["L0_ffn_w_gate"], f)
    w_up = np.asarray(inp["L0_ffn_w_up"], f)
    w_down = np.asarray(inp["L0_ffn_w_down"], f)

    m = {}
    m["wg_sl"] = ac(w_gate.reshape(16, 128, IC, C).transpose(2, 1, 0, 3).reshape(IC, 128, 16 * C))
    m["wu_sl"] = ac(w_up.reshape(16, 128, IC, C).transpose(2, 1, 0, 3).reshape(IC, 128, 16 * C))
    m["wd_sl"] = ac(w_down.reshape(IB, 128, DB, 128).transpose(2, 1, 0, 3).reshape(DB, 128, IB * 128))
    WskT = np.asarray(inp["L0_attn_W_skip"], f).T
    m["wsk_sl"] = ac(WskT.reshape(DB, 128, EB, 128).transpose(2, 1, 0, 3).reshape(EB, 128, DB * 128))
    WinT = np.asarray(inp["L0_attn_W_in"], f).T
    m["win_sl"] = ac(WinT.reshape(DB, 128, EB, 128).transpose(2, 1, 0, 3).reshape(EB, 128, DB * 128))
    WoT = np.asarray(inp["L0_attn_W_out"], f).T
    m["wo_sl"] = ac(WoT.reshape(EB, 128, DB, 128).transpose(2, 1, 0, 3).reshape(DB, 128, EB * 128))
    m["wd1_in"] = ac(np.asarray(inp["L0_attn_Wd1"], f).T.reshape(EB, 128, 128).transpose(1, 0, 2).reshape(128, EB * 128))
    m["wd2_in"] = ac(np.asarray(inp["L0_attn_Wd2"], f).T)
    m["wb_in"] = ac(np.asarray(inp["L0_attn_WB"], f).T.reshape(EB, 128, N).transpose(1, 0, 2).reshape(128, EB * N))
    m["wc_in"] = ac(np.asarray(inp["L0_attn_WC"], f).T.reshape(EB, 128, N).transpose(1, 0, 2).reshape(128, EB * N))
    m["cw_in"] = ac(np.asarray(inp["L0_attn_conv_w"], f).reshape(EB, 128, 4).transpose(1, 0, 2).reshape(128, EB * 4))
    m["cb_in"] = ac(np.asarray(inp["L0_attn_conv_b"], f).reshape(EB, 128).T)
    m["bd2_in"] = ac(np.asarray(inp["L0_attn_bd2"], f).reshape(EB, 128).T)
    m["wD_in"] = ac(np.asarray(inp["L0_attn_W_D"], f).reshape(EB, 128).T)
    m["nw_in"] = ac(np.asarray(inp["L0_attn_norm_w"], f).reshape(DB, 128).T)
    m["alog_in"] = ac(np.asarray(inp["L0_attn_A_log"], f).reshape(EB, 128, N).transpose(1, 0, 2).reshape(128, EB * N))
    m["consts_in"] = np.tile(np.array([[EPS, -0.5, 1.0 / D, -1.0, 1.0, 0.0, 0.0, 0.0]], f), (128, 1))
    m["ident_in"] = np.eye(128, dtype=f).astype(ml_dtypes.bfloat16)
    m["onescol_in"] = np.ones((128, 1), f).astype(ml_dtypes.bfloat16)
    m["onesrow_in"] = np.ones((1, 128), f)
    m["onesrowb_in"] = np.ones((1, 128), f).astype(ml_dtypes.bfloat16)
    return m


def kernel(**inputs):
    x = np.asarray(inputs["L0_attn_x"], np.float32)
    B = x.shape[0]
    x_fm = np.ascontiguousarray(x.transpose(0, 2, 1))
    shared = _prep_inputs(inputs)
    nc = build_program()
    in_maps = []
    for b in range(B):
        im = dict(shared)
        im["x_fm"] = x_fm[b]
        in_maps.append(im)
    res = run_bass_kernel_spmd(nc, in_maps, core_ids=list(range(B)))
    out = np.stack([res.results[b]["out"] for b in range(B)], axis=0)
    return np.ascontiguousarray(out.transpose(0, 2, 1))


if __name__ == "__main__":
    nc = build_program()
    print("instructions:", nc.n_instructions())
    print("sbuf bytes remaining:", nc.sbuf_bytes_remaining)


# version 20
# speedup vs baseline: 1.0670x; 1.0670x over previous
import numpy as np
import ml_dtypes
import concourse.bass as bass
import concourse.mybir as mybir
from concourse.bass_utils import run_bass_kernel_spmd
from contextlib import ExitStack

F32 = mybir.dt.float32
BF16 = mybir.dt.bfloat16
AF = mybir.ActivationFunctionType
ALU = mybir.AluOpType

D, L, E, I, N = 2048, 2048, 4096, 8192, 16
C = 512
NCH = L // C
DB, EB, IB, IC, TB = D // 128, E // 128, I // 128, I // C, C // 128
EPS = 1e-5

ENGS = ("tensor", "vector", "scalar", "gpsimd", "sync")
DEBUG = False


class KB:
    def __init__(self, nc, stack):
        self.nc = nc
        self.stack = stack
        self.ops = {e: [] for e in ENGS}
        self.sem = {}
        self.cnt = {}
        self.hw = {e: {} for e in ENGS}

    def mksem(self, *names):
        for name in names:
            self.sem[name] = self.stack.enter_context(self.nc.semaphore(name))
            self.cnt[name] = 0

    def wait(self, eng, sname, val=None):
        val = self.cnt[sname] if val is None else val
        if val <= 0 or val <= self.hw[eng].get(sname, 0):
            return
        self.hw[eng][sname] = val
        h = self.sem[sname]
        self.ops[eng].append(lambda X, h=h, v=val: X.wait_ge(h, v))

    def emit(self, eng, fn, inc=None, add=1):
        if inc is None:
            self.ops[eng].append(lambda X, fn=fn: fn(X))
            return None
        self.cnt[inc] += add
        h = self.sem[inc]
        self.ops[eng].append(lambda X, fn=fn, h=h, a=add: fn(X).then_inc(h, a))
        return self.cnt[inc]

    DMA_OWNERS = {"gpsimd": ("s_x", "s_bx", "s_sm", "wsA", "wsB", "wpA", "wpB", "wpC", "wpD",
                             "s_us", "s_ss", "s_xr"),
                  "sync": ("s_st", "s_out", "s_bw",
                           "wsA", "wsB", "wpA", "wpB", "wpC", "wpD",
                           "s_xl", "s_ud", "s_sd")}

    def barrier(self):
        for e, sems in KB.DMA_OWNERS.items():
            for s in sems:
                self.wait(e, s)
        tgt = self.cnt["ph"] + len(ENGS)
        h = self.sem["ph"]
        for e in ENGS:
            self.ops[e].append(lambda X, h=h: X.sem_inc(h, 1))
        self.cnt["ph"] = tgt
        for e in ENGS:
            self.wait(e, "ph", tgt)


def build_program():
    nc = bass.Bass("TRN2", target_bir_lowering=False, dynamic_dma_scratch_size=16384)
    stack = ExitStack()
    k = KB(nc, stack)
    k.mksem("ph", "s_x", "s_st", "s_out", "s_bw", "s_bx",
            "wsA", "wsB", "wpA", "wpB", "wpC", "wpD", "s_sm",
            "c_pe", "c_dv", "c_sc", "c_gp",
            "s_xl", "s_ud", "s_sd", "s_us", "s_ss", "s_xr")

    def din(name, shape, dt=F32):
        return nc.dram_tensor(name, shape, dt, kind="ExternalInput").ap()

    x_fm = din("x_fm", [D, L])
    wg_sl = din("wg_sl", [IC, 128, 16 * C])
    wu_sl = din("wu_sl", [IC, 128, 16 * C])
    wd_sl = din("wd_sl", [DB, 128, IB * 128])
    wsk_sl = din("wsk_sl", [EB, 128, DB * 128])
    win_sl = din("win_sl", [EB, 128, DB * 128])
    wo_sl = din("wo_sl", [DB, 128, EB * 128])
    wd1_in = din("wd1_in", [128, EB * 128])
    wd2_in = din("wd2_in", [128, E])
    wb_in = din("wb_in", [128, EB * N])
    wc_in = din("wc_in", [128, EB * N])
    cw_in = din("cw_in", [128, EB * 4])
    cb_in = din("cb_in", [128, EB])
    bd2_in = din("bd2_in", [128, EB])
    wD_in = din("wD_in", [128, EB])
    nw_in = din("nw_in", [128, DB])
    alog_in = din("alog_in", [128, EB * N])
    consts_in = din("consts_in", [128, 8])
    ident_in = nc.dram_tensor("ident_in", [128, 128], BF16, kind="ExternalInput").ap()
    onescol_in = nc.dram_tensor("onescol_in", [128, 1], BF16, kind="ExternalInput").ap()
    onesrow_in = din("onesrow_in", [1, 128])
    onesrowb_in = nc.dram_tensor("onesrowb_in", [1, 128], BF16, kind="ExternalInput").ap()

    out_d = nc.dram_tensor("out", [D, L], F32, kind="ExternalOutput").ap()
    _scr = "ExternalOutput" if DEBUG else "Internal"
    if DEBUG:
        dbg_xn = nc.dram_tensor("dbg_xn", [NCH, 128, DB * C], BF16, kind="ExternalOutput").ap()
        dbg_hb = nc.dram_tensor("dbg_hb", [NCH, 128, C], F32, kind="ExternalOutput").ap()
        dbg_ab = nc.dram_tensor("dbg_ab", [NCH, 128, C], F32, kind="ExternalOutput").ap()
        dbg_s14 = nc.dram_tensor("dbg_s14", [NCH, 128, C], F32, kind="ExternalOutput").ap()
        dbg_hsp = nc.dram_tensor("dbg_hsp", [NCH, 128, C], BF16, kind="ExternalOutput").ap()
        dbg_hspt = nc.dram_tensor("dbg_hspt", [NCH, 128, IB * C], BF16, kind="ExternalOutput").ap()
        dbg_scl = nc.dram_tensor("dbg_scl", [NCH, 128, C], F32, kind="ExternalOutput").ap()
        dbg_u = nc.dram_tensor("dbg_u", [NCH, 128, EB * C], BF16, kind="ExternalOutput").ap()
        dbg_vt = nc.dram_tensor("dbg_vt", [NCH, 128, EB * 3], F32, kind="ExternalOutput").ap()
        dbg_row = nc.dram_tensor("dbg_row", [NCH, 4, C], F32, kind="ExternalOutput").ap()
    stream1 = nc.dram_tensor("stream1", [NCH, 128, DB * C], BF16, kind=_scr).ap()
    stream2 = nc.dram_tensor("stream2", [NCH, 128, DB * C], BF16, kind=_scr).ap()
    bounce = nc.dram_tensor("bounce", [NCH, 2, N, C], BF16, kind=_scr).ap()
    wgb_d = nc.dram_tensor("wgb_d", [IC, 128, 16 * C], BF16, kind="Internal").ap()
    wub_d = nc.dram_tensor("wub_d", [IC, 128, 16 * C], BF16, kind="Internal").ap()
    wdb_d = nc.dram_tensor("wdb_d", [DB, 128, IB * 128], BF16, kind="Internal").ap()
    wskb_d = nc.dram_tensor("wskb_d", [EB, 128, DB * 128], BF16, kind="Internal").ap()
    winb_d = nc.dram_tensor("winb_d", [EB, 128, DB * 128], BF16, kind="Internal").ap()
    wob_d = nc.dram_tensor("wob_d", [DB, 128, EB * 128], BF16, kind="Internal").ap()
    u_d = nc.dram_tensor("u_d", [NCH, 128, EB * C], BF16, kind="Internal").ap()
    sk_d = nc.dram_tensor("sk_d", [NCH, 128, EB * C], BF16, kind="Internal").ap()

    RA = lambda name, shape, dt: stack.enter_context(nc.sbuf_tensor(name, shape, dt))
    ident = RA("ident", [128, 128], BF16)
    onescol = RA("onescol", [128, 1], BF16)
    onesrow = RA("onesrow", [1, 128], F32)
    onesrowb = RA("onesrowb", [1, 128], BF16)
    consts = RA("consts", [128, 8], F32)
    convw = RA("convw", [128, EB * 4], F32)
    convb = RA("convb", [128, EB], F32)
    bd2c = RA("bd2c", [128, EB], F32)
    wDc = RA("wDc", [128, EB], F32)
    nwc = RA("nwc", [128, DB], F32)
    aneg = RA("aneg", [128, EB * N], F32)
    carry = RA("carry", [128, EB * N], F32)
    vtail = RA("vtail", [128, EB * 3], F32)
    d1bf = RA("d1bf", [128, C], BF16)
    wbt = RA("wbt", [128, EB * N], BF16)
    wct = RA("wct", [128, EB * N], BF16)
    s1f = RA("s1f", [1, C], F32)
    sclf = RA("sclf", [1, C], F32)
    t1f = RA("t1f", [1, C], F32)
    t2f = RA("t2f", [1, C], F32)
    mean_rep = RA("mean_rep", [128, C], F32)
    scl_rep = RA("scl_rep", [128, C], F32)

    CE = lambda j: consts[:, j:j + 1]
    CE1 = lambda j: consts[0:1, j:j + 1]
    g = "gpsimd"

    for dst, src in [(ident, ident_in), (onescol, onescol_in), (onesrow, onesrow_in),
                     (onesrowb, onesrowb_in), (consts, consts_in), (convw, cw_in),
                     (convb, cb_in), (bd2c, bd2_in), (wDc, wD_in), (nwc, nw_in),
                     (wbt, wb_in), (wct, wc_in), (carry, alog_in)]:
        k.emit(g, lambda X, d=dst, s=src: X.dma_start(d[:], s), "s_sm", 16)
    k.wait("scalar", "s_sm")
    k.emit("scalar", lambda X: X.activation(aneg[:], carry[:], AF.Exp), "c_sc")
    k.wait("vector", "c_sc")
    k.emit("vector", lambda X: X.tensor_single_scalar(aneg[:], aneg[:], -1.0, ALU.mult), "c_dv")
    k.emit("vector", lambda X: X.memset(carry[:], 0.0), "c_dv")
    k.emit("vector", lambda X: X.memset(vtail[:], 0.0), "c_dv")
    k.barrier()

    def ln_stats(xck, sq, nslot, ps_a1, ps_b1, ps_r1, ps_r2, rms_extra):
        k.wait("tensor", "c_dv")
        k.wait("tensor", "c_sc")
        sq_tok = [0] * DB
        for db in range(DB):
            sl = db % nslot
            if db >= nslot:
                k.wait("scalar", "c_pe", sq_tok[db - nslot])
            k.emit("scalar", lambda X, db=db, sl=sl: X.activation(
                sq[:, sl * C:(sl + 1) * C], xck[:, db * C:(db + 1) * C], AF.Square), "c_sc")
            k.wait("tensor", "c_sc")
            k.emit("tensor", lambda X, db=db: X.matmul(
                ps_a1, onescol[:], xck[:, db * C:(db + 1) * C],
                start=(db == 0), stop=(db == DB - 1)), "c_pe")
            sq_tok[db] = k.emit("tensor", lambda X, db=db, sl=sl: X.matmul(
                ps_b1, onescol[:], sq[:, sl * C:(sl + 1) * C],
                start=(db == 0), stop=(db == DB - 1)), "c_pe")
        k.wait("vector", "c_pe")
        k.emit("vector", lambda X: X.tensor_single_scalar(s1f[:], ps_a1, 1.0 / D, ALU.mult), "c_dv")
        k.emit("vector", lambda X: X.tensor_single_scalar(t2f[:], ps_b1, 1.0 / D, ALU.mult), "c_dv")
        k.emit("vector", lambda X: X.tensor_mul(t1f[:], s1f[:], s1f[:]), "c_dv")
        k.emit("vector", lambda X: X.tensor_sub(t1f[:], t2f[:], t1f[:]), "c_dv")
        k.wait("scalar", "c_dv")
        k.emit("scalar", lambda X: X.activation(t2f[:], t1f[:], AF.Ln, bias=CE1(0)), "c_sc")
        k.emit("scalar", lambda X: X.activation(sclf[:], t2f[:], AF.Exp, scale=CE1(1)), "c_sc")
        if rms_extra:
            k.emit("scalar", lambda X: X.activation(t2f[:], t2f[:], AF.Exp, scale=CE1(3)), "c_sc")
            k.wait("vector", "c_sc")
            k.emit("vector", lambda X: X.tensor_mul(t2f[:], t2f[:], t1f[:]), "c_dv")
            k.wait("scalar", "c_dv")
            k.emit("scalar", lambda X: X.activation(t2f[:], t2f[:], AF.Ln, bias=CE1(0)), "c_sc")
            k.emit("scalar", lambda X: X.activation(t2f[:], t2f[:], AF.Exp, scale=CE1(1)), "c_sc")
            k.wait("vector", "c_sc")
            k.emit("vector", lambda X: X.tensor_mul(sclf[:], sclf[:], t2f[:]), "c_dv")
        k.wait("tensor", "c_sc")
        k.wait("tensor", "c_dv")
        k.emit("tensor", lambda X: X.matmul(ps_r1, onesrow[:], s1f[:], start=True, stop=True), "c_pe")
        k.emit("tensor", lambda X: X.matmul(ps_r2, onesrow[:], sclf[:], start=True, stop=True), "c_pe")
        k.wait("vector", "c_pe")
        k.emit("vector", lambda X: X.tensor_copy(mean_rep[:], ps_r1), "c_dv")
        k.emit("vector", lambda X: X.tensor_copy(scl_rep[:], ps_r2), "c_dv")
        return k.cnt["c_dv"]

    def ln_apply(xck, xnck, tmp, rep_tok, norm_w):
        k.wait("vector", "c_dv", rep_tok)
        for db in range(DB):
            k.emit("vector", lambda X, db=db: X.tensor_sub(
                tmp[:], xck[:, db * C:(db + 1) * C], mean_rep[:]), "c_dv")
            if norm_w:
                k.emit("vector", lambda X, db=db: X.scalar_tensor_tensor(
                    xnck[:, db * C:(db + 1) * C], tmp[:], nwc[:, db:db + 1], scl_rep[:],
                    ALU.mult, ALU.mult), "c_dv")
            else:
                k.emit("vector", lambda X, db=db: X.tensor_mul(
                    xnck[:, db * C:(db + 1) * C], tmp[:], scl_rep[:]), "c_dv")
        return k.cnt["c_dv"]

    def ffn_phase(first, pfx):
        ps = ExitStack()
        PA = lambda name, shape, dt: ps.enter_context(nc.sbuf_tensor(pfx + name, shape, dt))
        PP = lambda name, shape, dt=F32: ps.enter_context(nc.psum_tensor(pfx + name, shape, dt))
        ps_g = PP("ps_g", [128, C])
        ps_u = PP("ps_u", [128, C])
        ps_tr = PP("ps_tr", [128, C], BF16)
        ps_dA = PP("ps_dA", [128, C])
        ps_dB = PP("ps_dB", [128, C])
        ps_s1 = PP("ps_s1", [1, C])
        ps_s2 = PP("ps_s2", [1, C])
        ps_s3 = PP("ps_s3", [128, C])
        xck = PA("xck", [128, DB * C], BF16)
        xnck = PA("xnck", [128, DB * C], BF16)
        hspt = PA("hspt", [128, IB * C], BF16)
        wsbA = PA("wsbA", [128, 2 * 16 * C], BF16)
        wsbB = PA("wsbB", [128, 2 * 16 * C], BF16)
        sq2 = PA("sq2", [128, 2 * C], BF16)
        silg = PA("silg", [128, C], F32)
        hb = PA("hb", [128, C], F32)
        ab = PA("ab", [128, C], F32)
        mn4 = PA("mn4", [128, 4 * 128], F32)
        s14k = PA("s14k", [128, 4 * 128], F32)
        s14 = s14k
        kp = s14k[:, 384:512]
        hsp = PA("hsp", [128, 2 * C], BF16)
        ob = None if first else PA("ob", [128, 2 * C], F32)

        ws = [wsbA, wsbB]
        wsem = ["wsA", "wsB"]
        ws_tok = [0, 0]
        psd = [ps_dA, ps_dB]
        psd_tok = [0, 0]
        psg2 = [ps_g, ps_s3]
        psu2 = [ps_u, ps_dB]
        psg_tok2 = [0, 0]
        psu_tok2 = [0, 0]
        psg_tok = psu_tok = pstr_tok = 0
        hspdone_prev = 0
        hsp_tok = [0, 0]
        ob_tok = [0, 0]
        resid_done = 0
        pend = []
        gu_last = 0
        st_tok = 0

        for c in range(NCH):
            k.wait(g, "c_dv", resid_done)
            if first:
                src = x_fm[:, c * C:(c + 1) * C].rearrange("(db p) ii -> p db ii", p=128)
                k.emit(g, lambda X, src=src: X.dma_start(
                    xck[:].rearrange("p (db ii) -> p db ii", db=DB), src), "s_x", 16)
            else:
                k.emit(g, lambda X, c=c: X.dma_start(xck[:], stream2[c]), "s_x", 16)
            xtok = k.cnt["s_x"]
            for e in ("scalar", "tensor", "vector"):
                k.wait(e, "s_x", xtok)

            rep_tok = ln_stats(xck, sq2, 2, ps_s1[:], ps_s2[:], ps_s3[:], ps_dA[:],
                               rms_extra=False)
            k.wait("vector", "c_pe", gu_last)
            if first:
                k.wait("vector", "s_st", st_tok)
            apply_tok = ln_apply(xck, xnck, hb, rep_tok, norm_w=False)
            if DEBUG and first:
                k.wait("sync", "c_dv", apply_tok)
                k.emit("sync", lambda X, c=c: X.dma_start(dbg_xn[c], xnck[:]), "s_st", 16)
                k.emit("sync", lambda X, c=c: X.dma_start(dbg_scl[c], scl_rep[:]), "s_st", 16)
                k.emit("sync", lambda X, c=c: X.dma_start(dbg_row[c, 0:1], s1f[:]), "s_st", 16)
                k.emit("sync", lambda X, c=c: X.dma_start(dbg_row[c, 1:2], sclf[:]), "s_st", 16)
                k.emit("sync", lambda X, c=c: X.dma_start(dbg_row[c, 2:3], t1f[:]), "s_st", 16)
                k.emit("sync", lambda X, c=c: X.dma_start(dbg_row[c, 3:4], t2f[:]), "s_st", 16)

            for ic in range(IC):
                sl = ic % 2
                if first and c == 0:
                    k.wait(g, wsem[sl])
                    k.wait(g, "c_pe", ws_tok[sl])
                    k.emit(g, lambda X, ic=ic, sl=sl: X.dma_start(
                        ws[sl][:, 0:16 * C], wg_sl[ic]), wsem[sl], 16)
                    k.emit(g, lambda X, ic=ic, sl=sl: X.dma_start(
                        ws[sl][:, 16 * C:32 * C], wu_sl[ic]), wsem[sl], 16)
                    wtok = k.cnt[wsem[sl]]
                    k.wait("sync", wsem[sl], wtok)
                    k.emit("sync", lambda X, ic=ic, sl=sl: X.dma_start(
                        wgb_d[ic], ws[sl][:, 0:16 * C]), wsem[sl], 16)
                    k.emit("sync", lambda X, ic=ic, sl=sl: X.dma_start(
                        wub_d[ic], ws[sl][:, 16 * C:32 * C]), wsem[sl], 16)
                else:
                    k.wait("sync", wsem[sl])
                    k.wait("sync", "c_pe", ws_tok[sl])
                    k.emit("sync", lambda X, ic=ic, sl=sl: X.dma_start(
                        ws[sl][:, 0:16 * C], wgb_d[ic]), wsem[sl], 16)
                    k.emit("sync", lambda X, ic=ic, sl=sl: X.dma_start(
                        ws[sl][:, 16 * C:32 * C], wub_d[ic]), wsem[sl], 16)
                    wtok = k.cnt[wsem[sl]]
                k.wait("tensor", wsem[sl], wtok)
                k.wait("tensor", "c_dv", apply_tok)
                for tb in range(TB):
                    gp = (ic * TB + tb) % 2
                    psg = psg2[gp]
                    psu = psu2[gp]
                    k.wait("tensor", "c_sc", psg_tok2[gp])
                    for kb in range(DB):
                        k.emit("tensor", lambda X, kb=kb, tb=tb, sl=sl, psg=psg: X.matmul(
                            psg[:], xnck[:, kb * C + tb * 128: kb * C + tb * 128 + 128],
                            ws[sl][:, kb * C:(kb + 1) * C],
                            start=(kb == 0), stop=(kb == DB - 1)), "c_pe")
                    gdone = k.cnt["c_pe"]
                    k.wait("tensor", "c_dv", psu_tok2[gp])
                    k.wait("tensor", "c_dv", psd_tok[1])
                    for kb in range(DB):
                        k.emit("tensor", lambda X, kb=kb, tb=tb, sl=sl, psu=psu: X.matmul(
                            psu[:], xnck[:, kb * C + tb * 128: kb * C + tb * 128 + 128],
                            ws[sl][:, 16 * C + kb * C: 16 * C + (kb + 1) * C],
                            start=(kb == 0), stop=(kb == DB - 1)), "c_pe")
                    udone = k.cnt["c_pe"]
                    if tb == TB - 1:
                        ws_tok[sl] = udone
                        if ic == IC - 1:
                            gu_last = udone
                    while len(pend) > 1:
                        hsl_, ic_, tb_, tok_ = pend.pop(0)
                        k.wait("tensor", "c_dv", tok_)
                        k.wait("tensor", "c_sc", pstr_tok)
                        for q in range(4):
                            k.emit("tensor", lambda X, q=q, hsl_=hsl_: X.transpose(
                                ps_tr[:, q * 128:(q + 1) * 128],
                                hsp[:, hsl_ * C + q * 128: hsl_ * C + (q + 1) * 128],
                                ident[:]), "c_pe")
                        hsp_tok[hsl_] = k.cnt["c_pe"]
                        k.wait("scalar", "c_pe", hsp_tok[hsl_])
                        dst = hspt[:].rearrange("p (ib ii) -> p ib ii", ib=IB)
                        pstr_tok = k.emit("scalar", lambda X, dst=dst, ic_=ic_, tb_=tb_: X.copy(
                            dst[:, ic_ * 4:(ic_ + 1) * 4, tb_ * 128:(tb_ + 1) * 128],
                            ps_tr[:].rearrange("p (q ii) -> p q ii", q=4)), "c_sc")
                    k.wait("scalar", "c_pe", gdone)
                    k.wait("scalar", "c_dv", psu_tok)
                    psg_tok = k.emit("scalar", lambda X, psg=psg: X.activation(
                        silg[:], psg[:], AF.Silu), "c_sc")
                    psg_tok2[gp] = psg_tok
                    k.wait("vector", "c_pe", udone)
                    k.wait("vector", "c_sc", psg_tok)
                    k.emit("vector", lambda X, psu=psu: X.tensor_mul(hb[:], silg[:], psu[:]), "c_dv")
                    psu_tok = k.cnt["c_dv"]
                    psu_tok2[gp] = psu_tok
                    k.wait("scalar", "c_dv", psu_tok)
                    abs_tok = k.emit("scalar", lambda X: X.activation(
                        ab[:], hb[:], AF.Abs), "c_sc")
                    k.wait("vector", "c_sc", abs_tok)
                    a3 = ab[:].rearrange("p (n q) -> p n q", q=4)
                    h3 = hb[:].rearrange("p (n q) -> p n q", q=4)
                    q1 = lambda t: t.rearrange("p (n q) -> p n q", q=1)
                    k.emit("vector", lambda X, a3=a3: X.tensor_tensor(
                        q1(mn4[:, 0:128]), a3[:, :, 0:1], a3[:, :, 1:2], ALU.max), "c_dv")
                    k.emit("vector", lambda X, a3=a3: X.tensor_tensor(
                        q1(mn4[:, 128:256]), a3[:, :, 0:1], a3[:, :, 1:2], ALU.min), "c_dv")
                    k.emit("vector", lambda X, a3=a3: X.tensor_tensor(
                        q1(mn4[:, 256:384]), a3[:, :, 2:3], a3[:, :, 3:4], ALU.max), "c_dv")
                    k.emit("vector", lambda X, a3=a3: X.tensor_tensor(
                        q1(mn4[:, 384:512]), a3[:, :, 2:3], a3[:, :, 3:4], ALU.min), "c_dv")
                    k.emit("vector", lambda X: X.tensor_tensor(
                        s14[:, 0:128], mn4[:, 0:128], mn4[:, 256:384], ALU.min), "c_dv")
                    k.emit("vector", lambda X: X.tensor_tensor(
                        s14[:, 128:256], mn4[:, 128:256], mn4[:, 384:512], ALU.max), "c_dv")
                    k.emit("vector", lambda X: X.tensor_tensor(
                        s14[:, 256:384], s14[:, 0:128], s14[:, 128:256], ALU.max), "c_dv")
                    hsl = tb % 2
                    k.wait("vector", "c_pe", hsp_tok[hsl])
                    hs3 = hsp[:, hsl * C:(hsl + 1) * C].rearrange("p (n q) -> p n q", q=4)
                    for j in range(4):
                        k.emit("vector", lambda X, a3=a3, j=j: X.tensor_tensor(
                            q1(kp), a3[:, :, j:j + 1], q1(s14[:, 256:384]), ALU.is_ge), "c_dv")
                        k.emit("vector", lambda X, h3=h3, hs3=hs3, j=j: X.tensor_tensor(
                            hs3[:, :, j:j + 1], h3[:, :, j:j + 1], q1(kp), ALU.mult), "c_dv")
                    hspdone = k.cnt["c_dv"]
                    hspdone_prev = hspdone
                    pend.append((hsl, ic, tb, hspdone))

            def flush_tr():
                nonlocal pstr_tok
                while pend:
                    hsl_, ic_, tb_, tok_ = pend.pop(0)
                    k.wait("tensor", "c_dv", tok_)
                    k.wait("tensor", "c_sc", pstr_tok)
                    for q in range(4):
                        k.emit("tensor", lambda X, q=q, hsl_=hsl_: X.transpose(
                            ps_tr[:, q * 128:(q + 1) * 128],
                            hsp[:, hsl_ * C + q * 128: hsl_ * C + (q + 1) * 128],
                            ident[:]), "c_pe")
                    hsp_tok[hsl_] = k.cnt["c_pe"]
                    k.wait("scalar", "c_pe", hsp_tok[hsl_])
                    dst = hspt[:].rearrange("p (ib ii) -> p ib ii", ib=IB)
                    pstr_tok = k.emit("scalar", lambda X, dst=dst, ic_=ic_, tb_=tb_: X.copy(
                        dst[:, ic_ * 4:(ic_ + 1) * 4, tb_ * 128:(tb_ + 1) * 128],
                        ps_tr[:].rearrange("p (q ii) -> p q ii", q=4)), "c_sc")
            flush_tr()
            hspt_done = pstr_tok
            if DEBUG and first:
                k.wait("sync", "c_sc", hspt_done)
                k.emit("sync", lambda X, c=c: X.dma_start(dbg_hspt[c], hspt[:]), "s_st", 16)

            for db in range(DB):
                sl = db % 2
                if first and c == 0:
                    k.wait(g, wsem[sl])
                    k.wait(g, "c_pe", ws_tok[sl])
                    k.emit(g, lambda X, db=db, sl=sl: X.dma_start(
                        ws[sl][:, 0:IB * 128], wd_sl[db]), wsem[sl], 16)
                    wtok = k.cnt[wsem[sl]]
                    k.wait("sync", wsem[sl], wtok)
                    k.emit("sync", lambda X, db=db, sl=sl: X.dma_start(
                        wdb_d[db], ws[sl][:, 0:IB * 128]), wsem[sl], 16)
                else:
                    k.wait("sync", wsem[sl])
                    k.wait("sync", "c_pe", ws_tok[sl])
                    k.emit("sync", lambda X, db=db, sl=sl: X.dma_start(
                        ws[sl][:, 0:IB * 128], wdb_d[db]), wsem[sl], 16)
                    wtok = k.cnt[wsem[sl]]
                k.wait("tensor", wsem[sl], wtok)
                k.wait("tensor", "c_sc", hspt_done)
                k.wait("tensor", "c_dv", psd_tok[sl])
                if sl == 1:
                    k.wait("tensor", "c_dv", psu_tok2[1])
                for ib in range(IB):
                    k.emit("tensor", lambda X, ib=ib, sl=sl: X.matmul(
                        psd[sl][:], ws[sl][:, ib * 128:(ib + 1) * 128],
                        hspt[:, ib * C:(ib + 1) * C],
                        start=(ib == 0), stop=(ib == IB - 1)), "c_pe")
                ws_tok[sl] = k.cnt["c_pe"]
                k.wait("vector", "c_pe", ws_tok[sl])
                if first:
                    k.emit("vector", lambda X, db=db, sl=sl: X.scalar_tensor_tensor(
                        xnck[:, db * C:(db + 1) * C], psd[sl][:], 0.5,
                        xck[:, db * C:(db + 1) * C], ALU.mult, ALU.add), "c_dv")
                else:
                    osl = db % 2
                    k.wait("vector", "s_out", ob_tok[osl])
                    k.emit("vector", lambda X, db=db, sl=sl, osl=osl: X.scalar_tensor_tensor(
                        ob[:, osl * C:(osl + 1) * C], psd[sl][:], 0.5,
                        xck[:, db * C:(db + 1) * C], ALU.mult, ALU.add), "c_dv")
                    odone = k.cnt["c_dv"]
                    k.wait("sync", "c_dv", odone)
                    dsto = out_d[db * 128:(db + 1) * 128, c * C:(c + 1) * C]
                    ob_tok[osl] = k.emit("sync", lambda X, dsto=dsto, osl=osl: X.dma_start(
                        dsto, ob[:, osl * C:(osl + 1) * C]), "s_out", 16)
                psd_tok[sl] = k.cnt["c_dv"]
            resid_done = k.cnt["c_dv"]
            if first:
                k.wait("sync", "c_dv", resid_done)
                st_tok = k.emit("sync", lambda X, c=c: X.dma_start(
                    stream1[c], xnck[:]), "s_st", 16)
        ps.close()

    ffn_phase(first=True, pfx="p1_")
    k.barrier()

    p2 = ExitStack()
    PA = lambda name, shape, dt: p2.enter_context(nc.sbuf_tensor(name, shape, dt))
    PP = lambda name, shape, dt=F32: p2.enter_context(nc.psum_tensor(name, shape, dt))
    ps_sk = PP("ps_sk", [128, C])
    ps_v = PP("ps_v", [128, C])
    ps_d1 = PP("ps_d1", [128, C])
    ps_bt = PP("ps_bt", [16, C])
    ps_ct = PP("ps_ct", [16, C])
    ps_m1 = PP("ps_m1", [128, C])
    ps_m2 = PP("ps_m2", [128, C])
    ps_m3 = PP("ps_m3", [1, C])

    xnb = PA("xnb", [128, DB * C], BF16)
    brep = PA("brep", [128, N * C], BF16)
    crep = PA("crep", [128, N * C], BF16)
    dch = PA("dch", [128, EB * C], BF16)
    ych = PA("ych", [128, EB * C], BF16)
    wq0 = PA("wq0", [128, EB * 128], BF16)
    wq1 = PA("wq1", [128, EB * 128], BF16)
    wq2 = PA("wq2", [128, DB * 128], BF16)
    wq3 = wq2
    wd1t = PA("wd1t", [128, EB * 128], BF16)
    wd2t = PA("wd2t", [128, E], BF16)
    k.emit(g, lambda X: X.dma_start(wd1t[:], wd1_in), "s_sm", 16)
    k.emit(g, lambda X: X.dma_start(wd2t[:], wd2_in), "s_sm", 16)
    k.wait("tensor", "s_sm")
    vstage = PA("vstage", [128, C + 3], F32)
    uca = PA("uca", [128, C], F32)
    sq2b = PA("sq2b", [128, 2 * C], BF16)
    tmpc = PA("tmpc", [128, C], F32)
    tmpd = PA("tmpd", [128, C], F32)
    dub = PA("dub", [128, C], BF16)
    abuf = PA("abuf", [128, C], F32)
    bbuf = PA("bbuf", [128, C], BF16)
    hbuf = PA("hbuf", [128, 2 * C], F32)
    zbuf = PA("zbuf", [128, 2 * C], BF16)
    ybufA = PA("ybufA", [128, C], BF16)
    ybufB = PA("ybufB", [128, C], BF16)
    brr = PA("brr", [1, 2 * C], BF16)
    xst = PA("xst", [128, 2 * C], BF16)
    ustg = PA("ustg", [128, 1 * C], BF16)
    skstg = PA("skstg", [128, 1 * C], BF16)
    u_s = PA("u_s", [128, 2 * C], BF16)
    sk_s = PA("sk_s", [128, 2 * C], BF16)
    x1r = PA("x1r", [128, 1 * C], BF16)
    x2s = PA("x2s", [128, 1 * C], BF16)
    btf = PA("btf", [16, C], BF16)
    ctf = PA("ctf", [16, C], BF16)

    wp = [wq0, wq1, wq2, wq2]
    wpsem = ["wpA", "wpB", "wpC", "wpD"]
    wp_tok = [0, 0, 0, 0]
    wv_tok = [0, 0, 0, 0]
    wv_tok2 = [0, 0]
    wout_ps_free = [0]
    brr_tok = [0, 0]
    sk_tok = 0
    v_tok = 0
    u_tok = 0
    ustg_mm = [0, 0]
    ustg_st = [0, 0]
    skstg_st = [0, 0]
    us_tok = [0, 0]
    ss_tok = [0, 0]
    xr_tok = [0, 0]
    x2_tok = [0, 0]
    xstp_tok = [0, 0]
    xstd_tok = [0, 0]
    scan_tok = [0, 0]
    z_tok2 = [0, 0]
    y_tok2 = [0, 0]
    carry_tok = [0, 0]
    ab2 = [abuf, tmpd]
    yb = [ybufA, ybufB]
    a1_mm_last = 0
    wout_last = 0
    d1ev_tok = 0
    dmm_tok = 0
    rep_done = 0
    scan_last_dv = 0
    conv_last_dv = 0
    dch_rdy = 0
    dch_eb = [0] * EB
    b_tok_n = [0] * N
    z_tok_n = [0] * N
    brep_rdy = 0
    crep_rdy = 0
    yp_done = [0] * NCH

    def emit_ln(c):
        if True:
            k.wait("tensor", "c_dv")
            k.wait("tensor", "c_sc")
            sqtok = [0] * DB
            for db in range(DB):
                sl = db % 2
                k.wait("sync", "c_pe", xstp_tok[sl])
                k.wait("sync", "c_dv", xstd_tok[sl])
                k.emit("sync", lambda X, c=c, db=db, sl=sl: X.dma_start(
                    xst[:, sl * C:(sl + 1) * C],
                    stream1[c][:, db * C:(db + 1) * C]), "s_xl", 16)
                xl = k.cnt["s_xl"]
                k.wait("scalar", "s_xl", xl)
                if db >= 2:
                    k.wait("scalar", "c_pe", sqtok[db - 2])
                k.emit("scalar", lambda X, sl=sl, db=db: X.activation(
                    sq2b[:, (db % 2) * C:(db % 2 + 1) * C],
                    xst[:, sl * C:(sl + 1) * C], AF.Square), "c_sc")
                k.wait("tensor", "s_xl", xl)
                k.wait("tensor", "c_sc")
                k.emit("tensor", lambda X, sl=sl, db=db: X.matmul(
                    ps_m3[:], onescol[:], xst[:, sl * C:(sl + 1) * C],
                    start=(db == 0), stop=(db == DB - 1)), "c_pe")
                sqtok[db] = k.emit("tensor", lambda X, db=db: X.matmul(
                    ps_m1[0:1, :], onescol[:], sq2b[:, (db % 2) * C:(db % 2 + 1) * C],
                    start=(db == 0), stop=(db == DB - 1)), "c_pe")
                xstp_tok[sl] = sqtok[db]
            k.wait("vector", "c_pe", sqtok[DB - 1])
            k.emit("vector", lambda X: X.tensor_single_scalar(s1f[:], ps_m3[:], 1.0 / D, ALU.mult), "c_dv")
            k.emit("vector", lambda X: X.tensor_single_scalar(t2f[:], ps_m1[0:1, :], 1.0 / D, ALU.mult), "c_dv")
            k.emit("vector", lambda X: X.tensor_mul(t1f[:], s1f[:], s1f[:]), "c_dv")
            k.emit("vector", lambda X: X.tensor_sub(t1f[:], t2f[:], t1f[:]), "c_dv")
            k.wait("scalar", "c_dv")
            k.emit("scalar", lambda X: X.activation(t2f[:], t1f[:], AF.Ln, bias=CE1(0)), "c_sc")
            k.emit("scalar", lambda X: X.activation(sclf[:], t2f[:], AF.Exp, scale=CE1(1)), "c_sc")
            k.emit("scalar", lambda X: X.activation(t2f[:], t2f[:], AF.Exp, scale=CE1(3)), "c_sc")
            k.wait("vector", "c_sc")
            k.emit("vector", lambda X: X.tensor_mul(t2f[:], t2f[:], t1f[:]), "c_dv")
            k.wait("scalar", "c_dv")
            k.emit("scalar", lambda X: X.activation(t2f[:], t2f[:], AF.Ln, bias=CE1(0)), "c_sc")
            k.emit("scalar", lambda X: X.activation(t2f[:], t2f[:], AF.Exp, scale=CE1(1)), "c_sc")
            k.wait("vector", "c_sc")
            k.emit("vector", lambda X: X.tensor_mul(sclf[:], sclf[:], t2f[:]), "c_dv")
            k.wait("tensor", "c_dv")
            k.emit("tensor", lambda X: X.matmul(ps_m1[:], onesrow[:], s1f[:], start=True, stop=True), "c_pe")
            bc_tok = k.emit("tensor", lambda X: X.matmul(ps_m2[:], onesrow[:], sclf[:], start=True, stop=True), "c_pe")
            k.wait("vector", "c_pe", bc_tok)
            k.emit("vector", lambda X: X.tensor_copy(mean_rep[:], ps_m1[:]), "c_dv")
            k.emit("vector", lambda X: X.tensor_copy(scl_rep[:], ps_m2[:]), "c_dv")
            k.wait("vector", "c_pe", a1_mm_last)
            for db in range(DB):
                sl = db % 2
                k.wait("sync", "c_pe", xstp_tok[sl])
                k.wait("sync", "c_dv", xstd_tok[sl])
                k.emit("sync", lambda X, c=c, db=db, sl=sl: X.dma_start(
                    xst[:, sl * C:(sl + 1) * C],
                    stream1[c][:, db * C:(db + 1) * C]), "s_xl", 16)
                xl = k.cnt["s_xl"]
                k.wait("vector", "s_xl", xl)
                k.emit("vector", lambda X, sl=sl: X.tensor_sub(
                    tmpc[:], xst[:, sl * C:(sl + 1) * C], mean_rep[:]), "c_dv")
                k.emit("vector", lambda X, db=db: X.scalar_tensor_tensor(
                    xnb[:, db * C:(db + 1) * C], tmpc[:], nwc[:, db:db + 1], scl_rep[:],
                    ALU.mult, ALU.mult), "c_dv")
                xstd_tok[sl] = k.cnt["c_dv"]
            return k.cnt["c_dv"]

    apply_tok = emit_ln(0)
    for c in range(NCH + 1):
        cm1 = c - 1
        for eb in range(EB):
            if c < NCH:
                sl = eb % 2
                if c == 0:
                    k.wait(g, wpsem[sl])
                    k.wait(g, "c_pe", wp_tok[sl])
                    k.emit(g, lambda X, eb=eb, sl=sl: X.dma_start(
                        wp[sl][:, 0:DB * 128], wsk_sl[eb]), wpsem[sl], 16)
                    sk_w = k.cnt[wpsem[sl]]
                    k.wait("sync", wpsem[sl], sk_w)
                    k.emit("sync", lambda X, eb=eb, sl=sl: X.dma_start(
                        wskb_d[eb], wp[sl][:, 0:DB * 128]), wpsem[sl], 16)
                else:
                    k.wait("sync", wpsem[sl])
                    k.wait("sync", "c_pe", wp_tok[sl])
                    k.emit("sync", lambda X, eb=eb, sl=sl: X.dma_start(
                        wp[sl][:, 0:DB * 128], wskb_d[eb]), wpsem[sl], 16)
                    sk_w = k.cnt[wpsem[sl]]
                if c == 0:
                    k.wait(g, wpsem[2])
                    k.wait(g, "c_pe", max(wp_tok[2], wp_tok[3]))
                    k.emit(g, lambda X, eb=eb, sl=sl: X.dma_start(
                        wp[2][:], win_sl[eb]), wpsem[2], 16)
                    in_w = k.cnt[wpsem[2]]
                    k.wait("sync", wpsem[2], in_w)
                    k.emit("sync", lambda X, eb=eb: X.dma_start(
                        winb_d[eb], wp[2][:]), wpsem[2], 16)
                else:
                    k.wait("sync", wpsem[2])
                    k.wait("sync", "c_pe", max(wp_tok[2], wp_tok[3]))
                    k.emit("sync", lambda X, eb=eb: X.dma_start(
                        wp[2][:], winb_d[eb]), wpsem[2], 16)
                    in_w = k.cnt[wpsem[2]]
                k.wait("tensor", wpsem[sl], sk_w)
                k.wait("tensor", "c_dv", apply_tok)
                k.wait("tensor", "c_dv", wout_ps_free[0])
                k.wait("tensor", "c_sc", sk_tok)
                for kb in range(DB):
                    k.emit("tensor", lambda X, kb=kb, sl=sl: X.matmul(
                        ps_sk[:], wp[sl][:, kb * 128:(kb + 1) * 128],
                        xnb[:, kb * C:(kb + 1) * C],
                        start=(kb == 0), stop=(kb == DB - 1)), "c_pe")
                wp_tok[sl] = k.cnt["c_pe"]
                k.wait("scalar", "c_pe", wp_tok[sl])
                k.wait("scalar", "s_sd", skstg_st[0])
                sk_tok = k.emit("scalar", lambda X, sl=sl: X.activation(
                    skstg[:, 0:C], ps_sk[:], AF.Silu), "c_sc")
                k.wait("sync", "c_sc", sk_tok)
                skstg_st[0] = k.emit("sync", lambda X, c=c, eb=eb: X.dma_start(
                    sk_d[c][:, eb * C:(eb + 1) * C],
                    skstg[:, 0:C]), "s_sd", 16)
                k.wait("tensor", wpsem[2], in_w)
                k.wait("tensor", "c_dv", v_tok)
                for kb in range(DB):
                    k.emit("tensor", lambda X, kb=kb: X.matmul(
                        ps_v[:], wp[2][:, kb * 128:(kb + 1) * 128],
                        xnb[:, kb * C:(kb + 1) * C],
                        start=(kb == 0), stop=(kb == DB - 1)), "c_pe")
                wp_tok[2] = k.cnt["c_pe"]
                k.wait("vector", "c_pe", wp_tok[2])
                k.emit("vector", lambda X, eb=eb: X.tensor_copy(
                    vstage[:, 0:3], vtail[:, eb * 3:(eb + 1) * 3]), "c_dv")
                k.emit("vector", lambda X, eb=eb: X.tensor_copy(
                    vtail[:, eb * 3:(eb + 1) * 3], ps_v[:, C - 3:C]), "c_dv")
                k.emit("vector", lambda X: X.tensor_copy(vstage[:, 3:C + 3], ps_v[:]), "c_dv")
                v_tok = k.cnt["c_dv"]
                k.wait("vector", "c_sc", u_tok)
                k.emit("vector", lambda X, eb=eb: X.tensor_scalar(
                    uca[:], vstage[:, 3:C + 3], convw[:, eb * 4 + 3:eb * 4 + 4], None,
                    ALU.mult), "c_dv")
                k.emit("vector", lambda X, eb=eb: X.scalar_tensor_tensor(
                    tmpc[:], vstage[:, 2:C + 2], convw[:, eb * 4 + 2:eb * 4 + 3], uca[:],
                    ALU.mult, ALU.add), "c_dv")
                k.emit("vector", lambda X, eb=eb: X.scalar_tensor_tensor(
                    uca[:], vstage[:, 1:C + 1], convw[:, eb * 4 + 1:eb * 4 + 2], tmpc[:],
                    ALU.mult, ALU.add), "c_dv")
                k.emit("vector", lambda X, eb=eb: X.scalar_tensor_tensor(
                    tmpc[:], vstage[:, 0:C], convw[:, eb * 4:eb * 4 + 1], uca[:],
                    ALU.mult, ALU.add), "c_dv")
                conv_last_dv = k.cnt["c_dv"]
                k.wait("scalar", "c_dv", conv_last_dv)
                k.wait("scalar", "c_pe", ustg_mm[0])
                k.wait("scalar", "s_ud", ustg_st[0])
                u_tok = k.emit("scalar", lambda X, eb=eb, sl=sl: X.activation(
                    ustg[:, 0:C], tmpc[:], AF.Silu,
                    bias=convb[:, eb:eb + 1]), "c_sc")
                k.wait("sync", "c_sc", u_tok)
                ustg_st[0] = k.emit("sync", lambda X, c=c, eb=eb: X.dma_start(
                    u_d[c][:, eb * C:(eb + 1) * C],
                    ustg[:, 0:C]), "s_ud", 16)
                k.wait("tensor", "c_sc", u_tok)
                k.emit("tensor", lambda X, eb=eb, sl=sl: X.matmul(
                    ps_d1[:], wd1t[:, eb * 128:(eb + 1) * 128],
                    ustg[:, 0:C],
                    start=(eb == 0), stop=(eb == EB - 1)), "c_pe")
                k.emit("tensor", lambda X, eb=eb, sl=sl: X.matmul(
                    ps_bt[:], wbt[:, eb * N:(eb + 1) * N],
                    ustg[:, 0:C],
                    start=(eb == 0), stop=(eb == EB - 1)), "c_pe")
                k.emit("tensor", lambda X, eb=eb, sl=sl: X.matmul(
                    ps_ct[:], wct[:, eb * N:(eb + 1) * N],
                    ustg[:, 0:C],
                    start=(eb == 0), stop=(eb == EB - 1)), "c_pe")
                ustg_mm[0] = k.cnt["c_pe"]
                if eb == EB - 1:
                    a1_mm_last = k.cnt["c_pe"]

            if c > 0:
                ssl = eb % 2
                k.wait(g, "c_dv", us_tok[ssl])
                k.emit(g, lambda X, cm1=cm1, eb=eb, ssl=ssl: X.dma_start(
                    u_s[:, ssl * C:(ssl + 1) * C],
                    u_d[cm1][:, eb * C:(eb + 1) * C]), "s_us", 16)
                ul = k.cnt["s_us"]
                k.wait(g, "c_dv", ss_tok[ssl])
                k.emit(g, lambda X, cm1=cm1, eb=eb, ssl=ssl: X.dma_start(
                    sk_s[:, ssl * C:(ssl + 1) * C],
                    sk_d[cm1][:, eb * C:(eb + 1) * C]), "s_ss", 16)
                skl = k.cnt["s_ss"]
                if eb == 0:
                    k.wait("vector", "s_bx", brep_rdy)
                    k.wait(g, "s_x", crep_rdy)
                k.wait("scalar", "c_dv", dch_eb[eb])
                k.wait("vector", "s_us", ul)
                k.emit("vector", lambda X, eb=eb, ssl=ssl: X.tensor_mul(
                    dub[:], dch[:, eb * C:(eb + 1) * C],
                    u_s[:, ssl * C:(ssl + 1) * C]), "c_dv")

                def scan_post(n):
                    hsl2 = n % 2
                    k.wait("vector", "c_gp", z_tok2[hsl2])
                    if n == 0:
                        k.emit("vector", lambda X, hsl2=hsl2: X.tensor_copy(
                            ybufA[:], zbuf[:, hsl2 * C:(hsl2 + 1) * C]), "c_dv")
                    else:
                        k.emit("vector", lambda X, n=n, hsl2=hsl2: X.tensor_add(
                            yb[n % 2][:], yb[1 - n % 2][:],
                            zbuf[:, hsl2 * C:(hsl2 + 1) * C]), "c_dv")
                    y_tok2[hsl2] = k.cnt["c_dv"]

                def z_emit(n, delay_tok):
                    asl = n % 2
                    k.wait(g, "c_dv", scan_tok[asl])
                    k.wait(g, "c_dv", delay_tok)
                    k.wait(g, "c_dv", y_tok2[asl])
                    z_tok2[asl] = k.emit(g, lambda X, n=n, asl=asl: X.tensor_tensor(
                        zbuf[:, asl * C:(asl + 1) * C], hbuf[:, asl * C:(asl + 1) * C],
                        crep[:, n * C:(n + 1) * C], ALU.mult), "c_gp")
                    if eb == EB - 1:
                        z_tok_n[n] = z_tok2[asl]

                def carry_emit(n):
                    asl = n % 2
                    col = eb * N + n
                    k.wait("scalar", "c_dv", scan_tok[asl])
                    carry_tok[asl] = k.emit("scalar", lambda X, col=col, asl=asl: X.copy(
                        carry[:, col:col + 1], hbuf[:, asl * C + C - 1: asl * C + C]), "c_sc")

                for n in range(N):
                    col = eb * N + n
                    asl = n % 2
                    k.wait("scalar", "c_dv", scan_tok[asl])
                    a_sc = k.emit("scalar", lambda X, eb=eb, col=col, asl=asl: X.activation(
                        ab2[asl][:], dch[:, eb * C:(eb + 1) * C], AF.Exp,
                        scale=aneg[:, col:col + 1]), "c_sc")
                    k.emit("vector", lambda X, n=n: X.tensor_mul(
                        bbuf[:], dub[:], brep[:, n * C:(n + 1) * C]), "c_dv")
                    bn_tok = k.cnt["c_dv"]
                    if eb == EB - 1:
                        b_tok_n[n] = bn_tok
                    k.wait("vector", "c_sc", a_sc)
                    k.wait("vector", "c_gp", z_tok2[asl])
                    k.wait("vector", "c_sc", carry_tok[asl])
                    scan_tok[asl] = k.emit("vector", lambda X, col=col, asl=asl: X.tensor_tensor_scan(
                        hbuf[:, asl * C:(asl + 1) * C], ab2[asl][:], bbuf[:],
                        carry[:, col:col + 1], ALU.mult, ALU.add), "c_dv")
                    if n > 0:
                        z_emit(n - 1, bn_tok)
                        carry_emit(n - 1)
                    if n > 1:
                        scan_post(n - 2)
                z_emit(N - 1, scan_tok[(N - 1) % 2])
                carry_emit(N - 1)
                scan_post(N - 2)
                scan_post(N - 1)
                ylast = yb[(N - 1) % 2]
                k.emit("vector", lambda X, eb=eb, ssl=ssl, ylast=ylast: X.scalar_tensor_tensor(
                    zbuf[:, 0:C], u_s[:, ssl * C:(ssl + 1) * C], wDc[:, eb:eb + 1],
                    ylast[:], ALU.mult, ALU.add), "c_dv")
                us_tok[ssl] = k.cnt["c_dv"]
                k.wait("vector", "s_ss", skl)
                k.wait("vector", "c_pe", wout_last)
                k.emit("vector", lambda X, eb=eb, ssl=ssl: X.tensor_mul(
                    ych[:, eb * C:(eb + 1) * C], zbuf[:, 0:C],
                    sk_s[:, ssl * C:(ssl + 1) * C]), "c_dv")
                ss_tok[ssl] = k.cnt["c_dv"]
                if eb == EB - 1:
                    scan_last_dv = k.cnt["c_dv"]
                    yp_done[cm1] = scan_last_dv

        if c < NCH:
            k.wait("scalar", "c_pe", a1_mm_last)
            k.wait("scalar", "c_pe", dmm_tok)
            d1ev_tok = k.emit("scalar", lambda X: X.copy(d1bf[:], ps_d1[:]), "c_sc")
            k.wait("vector", "c_pe", a1_mm_last)
            k.emit("vector", lambda X: X.tensor_copy(btf[:], ps_bt[:]), "c_dv")
            k.emit("vector", lambda X: X.tensor_copy(ctf[:], ps_ct[:]), "c_dv")
            bt_tok = k.cnt["c_dv"]
            k.wait("sync", "c_dv", bt_tok)
            k.emit("sync", lambda X, c=c: X.dma_start(bounce[c, 0], btf[:]), "s_bw", 16)
            k.emit("sync", lambda X, c=c: X.dma_start(bounce[c, 1], ctf[:]), "s_bw", 16)
            bw_tok = k.cnt["s_bw"]

        if c < NCH:
            for n in range(N):
                k.wait(g, "s_bw", bw_tok)
                k.wait(g, "c_dv", b_tok_n[n])
                k.emit(g, lambda X, c=c, n=n: X.dma_start(
                    brep[:, n * C:(n + 1) * C],
                    bounce[c, 0, n].partition_broadcast(128)), "s_bx", 16)
            brep_rdy = k.cnt["s_bx"]
            for n in range(N):
                k.wait(g, "s_bw", bw_tok)
                k.wait(g, "c_gp", z_tok_n[n])
                k.emit(g, lambda X, c=c, n=n: X.dma_start(
                    crep[:, n * C:(n + 1) * C],
                    bounce[c, 1, n].partition_broadcast(128)), "s_x", 16)
            crep_rdy = k.cnt["s_x"]

            dt0 = hbuf[:, 0:C]
            dt1 = hbuf[:, C:2 * C]
            sp_dv = 0
            for eb in range(EB):
                k.wait("tensor", "c_sc", d1ev_tok)
                k.wait("tensor", "c_sc")
                k.wait("tensor", "c_dv", v_tok)
                k.emit("tensor", lambda X, eb=eb: X.matmul(
                    ps_v[:], wd2t[:, eb * 128:(eb + 1) * 128], d1bf[:],
                    start=True, stop=True), "c_pe")
                dmm_tok = k.cnt["c_pe"]
                k.wait("scalar", "c_pe", dmm_tok)
                k.wait("scalar", "c_gp", max(z_tok2))
                k.wait("scalar", "c_dv", max(sp_dv, scan_tok[0], scan_tok[1]))
                k.emit("scalar", lambda X, eb=eb, dt0=dt0: X.activation(
                    dt0, ps_v[:], AF.Relu, bias=bd2c[:, eb:eb + 1]), "c_sc")
                k.emit("scalar", lambda X, eb=eb, dt1=dt1: X.activation(
                    dt1, ps_v[:], AF.Abs, bias=bd2c[:, eb:eb + 1]), "c_sc")
                k.emit("scalar", lambda X, dt1=dt1: X.activation(dt1, dt1, AF.Exp, scale=CE(3)), "c_sc")
                sp_sc = k.emit("scalar", lambda X, dt1=dt1: X.activation(dt1, dt1, AF.Ln, bias=CE(4)), "c_sc")
                k.wait("vector", "c_sc", sp_sc)
                k.emit("vector", lambda X, eb=eb, dt0=dt0, dt1=dt1: X.tensor_add(
                    dch[:, eb * C:(eb + 1) * C], dt0, dt1), "c_dv")
                sp_dv = k.cnt["c_dv"]
                dch_eb[eb] = sp_dv
            dch_rdy = sp_dv

        if c > 0:
            pso = [ps_m1, ps_m2, ps_sk, ps_v]
            pso_tok = [0, 0, 0, 0]
            k.wait("tensor", "c_sc", sk_tok)
            k.wait("tensor", "c_sc")
            k.wait("tensor", "c_dv")

            def wout_resid(db):
                sl = db % 4
                k.wait("sync", "c_dv", xr_tok[0])
                k.emit("sync", lambda X, cm1=cm1, db=db: X.dma_start(
                    x1r[:, 0:C],
                    stream1[cm1][:, db * C:(db + 1) * C]), "s_xr", 16)
                xrl = k.cnt["s_xr"]
                k.wait("vector", "c_pe", wv_tok[sl])
                k.wait("vector", "s_xr", xrl)
                k.wait("vector", "s_st", x2_tok[0])
                k.emit("vector", lambda X, db=db, sl=sl: X.scalar_tensor_tensor(
                    x2s[:, 0:C], pso[sl][:], 1.0,
                    x1r[:, 0:C], ALU.bypass, ALU.add), "c_dv")
                xr_tok[0] = k.cnt["c_dv"]
                pso_tok[sl] = k.cnt["c_dv"]
                k.wait("sync", "c_dv", pso_tok[sl])
                x2_tok[0] = k.emit("sync", lambda X, cm1=cm1, db=db: X.dma_start(
                    stream2[cm1][:, db * C:(db + 1) * C],
                    x2s[:, 0:C]), "s_st", 16)

            for db in range(DB):
                sl = db % 4
                wsl = db % 2
                if cm1 == 0:
                    k.wait(g, wpsem[wsl])
                    k.wait(g, "c_pe", wv_tok2[wsl])
                    k.emit(g, lambda X, db=db, wsl=wsl: X.dma_start(
                        wp[wsl][:], wo_sl[db]), wpsem[wsl], 16)
                    wtok = k.cnt[wpsem[wsl]]
                    k.wait("sync", wpsem[wsl], wtok)
                    k.emit("sync", lambda X, db=db, wsl=wsl: X.dma_start(
                        wob_d[db], wp[wsl][:]), wpsem[wsl], 16)
                else:
                    k.wait("sync", wpsem[wsl])
                    k.wait("sync", "c_pe", wv_tok2[wsl])
                    k.emit("sync", lambda X, db=db, wsl=wsl: X.dma_start(
                        wp[wsl][:], wob_d[db]), wpsem[wsl], 16)
                    wtok = k.cnt[wpsem[wsl]]
                k.wait("tensor", wpsem[wsl], wtok)
                k.wait("tensor", "c_dv", yp_done[cm1])
                k.wait("tensor", "c_dv", pso_tok[sl])
                for eb in range(EB):
                    k.emit("tensor", lambda X, eb=eb, sl=sl, wsl=wsl: X.matmul(
                        pso[sl][:], wp[wsl][:, eb * 128:(eb + 1) * 128],
                        ych[:, eb * C:(eb + 1) * C],
                        start=(eb == 0), stop=(eb == EB - 1)), "c_pe")
                wv_tok[sl] = k.cnt["c_pe"]
                wv_tok2[wsl] = k.cnt["c_pe"]
                if db == DB - 1:
                    wout_last = k.cnt["c_pe"]
                if db >= 3:
                    wout_resid(db - 3)
            for db in range(DB - 3, DB):
                wout_resid(db)
            wout_ps_free[0] = k.cnt["c_dv"]

        if c + 1 < NCH:
            apply_tok = emit_ln(c + 1)

    p2.close()
    k.barrier()

    ffn_phase(first=False, pfx="p3_")
    k.wait("sync", "s_out")
    k.barrier()

    with nc.Block() as block:
        block.tensor(lambda X: [f(X) for f in k.ops["tensor"]])
        block.vector(lambda X: [f(X) for f in k.ops["vector"]])
        block.scalar(lambda X: [f(X) for f in k.ops["scalar"]])
        block.gpsimd(lambda X: [f(X) for f in k.ops["gpsimd"]])
        block.sync(lambda X: [f(X) for f in k.ops["sync"]])
    stack.close()
    return nc


def _prep_inputs(inp):
    f = np.float32
    ac = np.ascontiguousarray
    w_gate = np.asarray(inp["L0_ffn_w_gate"], f)
    w_up = np.asarray(inp["L0_ffn_w_up"], f)
    w_down = np.asarray(inp["L0_ffn_w_down"], f)

    m = {}
    m["wg_sl"] = ac(w_gate.reshape(16, 128, IC, C).transpose(2, 1, 0, 3).reshape(IC, 128, 16 * C))
    m["wu_sl"] = ac(w_up.reshape(16, 128, IC, C).transpose(2, 1, 0, 3).reshape(IC, 128, 16 * C))
    m["wd_sl"] = ac(w_down.reshape(IB, 128, DB, 128).transpose(2, 1, 0, 3).reshape(DB, 128, IB * 128))
    WskT = np.asarray(inp["L0_attn_W_skip"], f).T
    m["wsk_sl"] = ac(WskT.reshape(DB, 128, EB, 128).transpose(2, 1, 0, 3).reshape(EB, 128, DB * 128))
    WinT = np.asarray(inp["L0_attn_W_in"], f).T
    m["win_sl"] = ac(WinT.reshape(DB, 128, EB, 128).transpose(2, 1, 0, 3).reshape(EB, 128, DB * 128))
    WoT = np.asarray(inp["L0_attn_W_out"], f).T
    m["wo_sl"] = ac(WoT.reshape(EB, 128, DB, 128).transpose(2, 1, 0, 3).reshape(DB, 128, EB * 128))
    m["wd1_in"] = ac(np.asarray(inp["L0_attn_Wd1"], f).T.reshape(EB, 128, 128).transpose(1, 0, 2).reshape(128, EB * 128))
    m["wd2_in"] = ac(np.asarray(inp["L0_attn_Wd2"], f).T)
    m["wb_in"] = ac(np.asarray(inp["L0_attn_WB"], f).T.reshape(EB, 128, N).transpose(1, 0, 2).reshape(128, EB * N))
    m["wc_in"] = ac(np.asarray(inp["L0_attn_WC"], f).T.reshape(EB, 128, N).transpose(1, 0, 2).reshape(128, EB * N))
    m["cw_in"] = ac(np.asarray(inp["L0_attn_conv_w"], f).reshape(EB, 128, 4).transpose(1, 0, 2).reshape(128, EB * 4))
    m["cb_in"] = ac(np.asarray(inp["L0_attn_conv_b"], f).reshape(EB, 128).T)
    m["bd2_in"] = ac(np.asarray(inp["L0_attn_bd2"], f).reshape(EB, 128).T)
    m["wD_in"] = ac(np.asarray(inp["L0_attn_W_D"], f).reshape(EB, 128).T)
    m["nw_in"] = ac(np.asarray(inp["L0_attn_norm_w"], f).reshape(DB, 128).T)
    m["alog_in"] = ac(np.asarray(inp["L0_attn_A_log"], f).reshape(EB, 128, N).transpose(1, 0, 2).reshape(128, EB * N))
    m["consts_in"] = np.tile(np.array([[EPS, -0.5, 1.0 / D, -1.0, 1.0, 0.0, 0.0, 0.0]], f), (128, 1))
    m["ident_in"] = np.eye(128, dtype=f).astype(ml_dtypes.bfloat16)
    m["onescol_in"] = np.ones((128, 1), f).astype(ml_dtypes.bfloat16)
    m["onesrow_in"] = np.ones((1, 128), f)
    m["onesrowb_in"] = np.ones((1, 128), f).astype(ml_dtypes.bfloat16)
    return m


def kernel(**inputs):
    x = np.asarray(inputs["L0_attn_x"], np.float32)
    B = x.shape[0]
    x_fm = np.ascontiguousarray(x.transpose(0, 2, 1))
    shared = _prep_inputs(inputs)
    nc = build_program()
    in_maps = []
    for b in range(B):
        im = dict(shared)
        im["x_fm"] = x_fm[b]
        in_maps.append(im)
    res = run_bass_kernel_spmd(nc, in_maps, core_ids=list(range(B)))
    out = np.stack([res.results[b]["out"] for b in range(B)], axis=0)
    return np.ascontiguousarray(out.transpose(0, 2, 1))


if __name__ == "__main__":
    nc = build_program()
    print("instructions:", nc.n_instructions())
    print("sbuf bytes remaining:", nc.sbuf_bytes_remaining)
